# version 09
# baseline (speedup 1.0000x reference)
import math
import numpy as np
import ml_dtypes
from contextlib import ExitStack

import concourse.bass as bass
import concourse.mybir as mybir
from concourse.bass_utils import run_bass_kernel_spmd

F32 = mybir.dt.float32
BF16 = mybir.dt.bfloat16
I32 = mybir.dt.int32
AF = mybir.ActivationFunctionType
OP = mybir.AluOpType
AX = mybir.AxisListType.X

S = 8192
H = 4096
D = 128
NKV = 8
G = 4
NQH = 32
NT = 16
TW = 512
OWN = 2 * TW
EPS = 1e-5
QSCALE = 1.0 / math.sqrt(D)
EXPB = -20.0
NCORE = 8
NTILE = S // 128

SLOT_CHUNKS = [4, 8, 16, 16, 24]
NSLOT = 5
REG_W = [(c - 1) * 128 + TW for c in SLOT_CHUNKS]
REG_OFF = [0]
for _w in REG_W[:-1]:
    REG_OFF.append(REG_OFF[-1] + _w)
REG_TOT = REG_OFF[-1] + REG_W[-1]
EMAX = max(REG_W)
KMAX = max(SLOT_CHUNKS)

A_SETS = {0: [0], 1: [1], 2: [0, 1], 3: [2], 4: [0, 2], 5: [1, 2],
          6: [0, 1, 2], 7: [1, 4]}
SLOTK_OFF = [0]
for _c in SLOT_CHUNKS[:-1]:
    SLOTK_OFF.append(SLOTK_OFF[-1] + _c)
NCHUNK = sum(SLOT_CHUNKS)


def _alibi_slopes(n):
    cp2 = 2 ** math.floor(math.log2(n))
    base = 2.0 ** (-8.0 / cp2)
    slopes = base ** np.arange(1, cp2 + 1, dtype=np.float64)
    if cp2 != n:
        eb = 2.0 ** (-4.0 / cp2)
        extra = eb ** np.arange(1, 1 + 2 * (n - cp2), 2, dtype=np.float64)
        slopes = np.concatenate([slopes, extra])
    return slopes


def _plan(c):
    a_set = A_SETS[c]
    tile_of = [0 if s in a_set else 1 for s in range(NSLOT)]
    kbase = []
    cum = [0, 0]
    for s in range(NSLOT):
        t = tile_of[s]
        kbase.append(cum[t])
        cum[t] += SLOT_CHUNKS[s]
    assert cum[0] == (c + 1) * 4 and cum[1] == (16 - c) * 4
    mask = [[1.0 if tile_of[s] == t else 0.0 for s in range(NSLOT)]
            for t in range(2)]
    return tile_of, kbase, mask


def _bake_etab(c):
    slopes = _alibi_slopes(NQH).astype(np.float64)
    tile_of, kbase, _ = _plan(c)
    tiles = (c, 15 - c)
    out = np.zeros((NQH, 128, REG_TOT), dtype=np.float32)
    p = np.arange(128, dtype=np.int64)[:, None]
    for s in range(NSLOT):
        qb = tiles[tile_of[s]] * TW
        kb = kbase[s] * 128
        X = np.arange(REG_W[s], dtype=np.int64)[None, :]
        rel = qb + X - (SLOT_CHUNKS[s] - 1) * 128 - kb - p
        sq = np.sqrt(np.maximum(rel, 0).astype(np.float64))
        valid = rel >= 0
        for h in range(NQH):
            v = np.where(valid, -slopes[h] * sq, -1e9)
            out[h][:, REG_OFF[s]:REG_OFF[s] + REG_W[s]] = v.astype(np.float32)
    return out


class Ctr:
    def __init__(self, sem, step=1):
        self.sem = sem
        self.step = step
        self.n = 0

    def inc(self, ins):
        self.n += self.step
        return ins.then_inc(self.sem, self.step)


def build_nc(stage=4):
    nc = bass.Bass("TRN2", target_bir_lowering=False)

    x_all = nc.dram_tensor("x_all", [S, H], F32, kind="ExternalInput").ap()
    x_own = nc.dram_tensor("x_own", [OWN, H], F32, kind="ExternalInput").ap()
    w_q = nc.dram_tensor("w_q", [H, NQH * D], F32, kind="ExternalInput").ap()
    w_k = nc.dram_tensor("w_k", [H, NKV * D], F32, kind="ExternalInput").ap()
    w_v = nc.dram_tensor("w_v", [H, NKV * D], F32, kind="ExternalInput").ap()
    w_o = nc.dram_tensor("w_o", [H, H], F32, kind="ExternalInput").ap()
    w_up = nc.dram_tensor("w_up", [H, H], F32, kind="ExternalInput").ap()
    gate_wt = nc.dram_tensor("gate_wt", [128, 128], F32, kind="ExternalInput").ap()
    gate_b = nc.dram_tensor("gate_b", [1, 4], F32, kind="ExternalInput").ap()
    lora_at = nc.dram_tensor("lora_at", [128, 1024], F32, kind="ExternalInput").ap()
    lora_bt = nc.dram_tensor("lora_bt", [32, H], F32, kind="ExternalInput").ap()
    etab = nc.dram_tensor("etab", [NQH, 128, REG_TOT], F32, kind="ExternalInput").ap()
    itab = nc.dram_tensor("itab", [1, 16], I32, kind="ExternalInput").ap()
    ftab = nc.dram_tensor("ftab", [128, 16], F32, kind="ExternalInput").ap()

    out_own = nc.dram_tensor("out_own", [OWN, H], F32, kind="ExternalOutput").ap()
    dbg = None
    if stage < 4:
        dbg = nc.dram_tensor("dbg", [128, 4 * H], F32, kind="ExternalOutput").ap()

    xn_tok = nc.dram_tensor("xn_tok", [S, H], BF16, kind="Internal")
    xn_own_d = nc.dram_tensor("xn_own_d", [OWN, H], BF16, kind="Internal")
    kt_d = nc.dram_tensor("kt_d", [NKV, 128, S], BF16, kind="Internal")
    v_d = nc.dram_tensor("v_d", [S // 128, 128, 2 * TW], BF16, kind="Internal")
    q_d = nc.dram_tensor("q_d", [NQH, 128, OWN], BF16, kind="Internal")

    ctx = ExitStack()
    ALL_CTRS = []

    def SEM(n, step=1):
        c = Ctr(ctx.enter_context(nc.semaphore(n)), step)
        ALL_CTRS.append(c)
        return c

    with ctx:
        itb = ctx.enter_context(nc.sbuf_tensor("itb", [1, 16], I32))
        ftb = ctx.enter_context(nc.sbuf_tensor("ftb", [128, 16], F32))
        ident = ctx.enter_context(nc.sbuf_tensor("ident", [128, 128], BF16))
        zcol = ctx.enter_context(nc.sbuf_tensor("zcol", [128, 1], F32))
        epscol = ctx.enter_context(nc.sbuf_tensor("epscol", [128, 1], F32))

        s_itb = SEM("s_itb", 16)
        s_x = [SEM("s_x0", 16), SEM("s_x1", 16)]
        s_xnw = [SEM("s_xnw0", 16), SEM("s_xnw1", 16)]
        s_xnt = [SEM("s_xnt0", 16), SEM("s_xnt1", 16)]
        s_wk = SEM("s_wk", 16)
        s_wv = SEM("s_wv", 16)
        s_wq = [SEM("s_wq0", 16), SEM("s_wq1", 16)]
        s_lnA = SEM("s_lnA")
        s_lnD = SEM("s_lnD")
        s_pe = SEM("s_pe")
        s_ev = SEM("s_ev")
        s_kvw = [SEM("s_kvw0", 16), SEM("s_kvw1", 16)]
        s_bnc = SEM("s_bnc", 16)
        s_gp = SEM("s_gp")
        s_out = SEM("s_out", 16)

        block = ctx.enter_context(nc.Block())

        @block.sync
        def _(sync):
            sync.dma_start(out=itb[:], in_=itab).then_inc(s_itb.sem, 16)
            sync.dma_start(out=ftb[:], in_=ftab).then_inc(s_itb.sem, 16)
        s_itb.n = 32

        @block.gpsimd
        def _(g):
            s_gp.inc(g.memset(zcol[:], 0.0))
            s_gp.inc(g.memset(epscol[:], EPS))
            s_gp.inc(g.memset(ident[:], 0.0))
            g.wait_ge(s_gp.sem, 3)
            s_gp.inc(g.affine_select(
                out=ident[:], in_=ident[:], compare_op=OP.not_equal,
                fill=1.0, base=0, pattern=[[-1, 128]], channel_multiplier=1))
        SETUP_GP = 4

        cnt2 = lambda n: 16 * (n // 2 + 1)
        xr_cnt = lambda gi: 16 * 32 * (gi // 2 + 1)

        def snap_ctrs():
            return [(c, c.n) for c in ALL_CTRS]

        def fence(eng, snap):
            for c, v in snap:
                if v > 0:
                    eng.wait_ge(c.sem, v)

        act_sched = []
        for ti in range(NTILE):
            act_sched.append(("sq", ti))
            act_sched.append(("sqrt", ti))
        ACT_IDX = {}
        for pos, (kind, ti) in enumerate(act_sched):
            ACT_IDX[(kind, ti)] = pos + 1

        LND = 7

        with ExitStack() as ph:
            psb = lambda n, s, d: ph.enter_context(nc.sbuf_tensor(n, s, d))
            wkb = psb("wkb", [128, 32 * NKV * D], BF16)
            xbuf = psb("xbuf", [128, 2 * H], F32)
            xnb = psb("xnb", [128, 2 * H], BF16)
            sqs = psb("sqs", [128, H], F32)
            stt = psb("stt", [128, 16], F32)
            xnt = psb("xnt", [128, 2 * 32 * TW], BF16)
            kvev = psb("kvev", [128, 2 * TW], BF16)
            kpsum = ph.enter_context(nc.psum_tensor("kpsum", [128, 8 * TW], F32))

            def stv(name, par):
                i = {"s1": 0, "ss": 1, "mean": 2, "m2": 3, "var": 4,
                     "r0": 5, "rstd": 6, "nmr": 7}[name] * 2 + par
                return stt[:, i:i + 1]

            def tread(eng, i, h0, h1):
                eng.wait_ge(s_xnw[0].sem, cnt2(4 * i + 2))
                eng.wait_ge(s_xnw[1].sem, cnt2(4 * i + 3))
                if i >= 2:
                    eng.wait_ge(s_pe.sem, (i - 1) * NKV)
                par = i % 2
                for hc in range(h0, h1):
                    s_xnt[par].inc(eng.dma_start_transpose(
                        xnt[:, (par * 32 + hc) * TW:(par * 32 + hc + 1) * TW],
                        xn_tok.ap()[i * TW:(i + 1) * TW, hc * 128:(hc + 1) * 128]))

            @block.sync
            def _(sync):
                for i in range(NT):
                    tread(sync, i, 0, 32)

            @block.vector
            def _(v):
                for ti in range(NTILE):
                    par = ti % 2
                    xa = xbuf[:, par * H:(par + 1) * H]
                    base = ti * LND
                    v.wait_ge(s_x[ti % 2].sem, cnt2(ti))
                    if ti >= 2:
                        v.wait_ge(s_lnD.sem, (ti - 2) * LND + 2)
                    s_lnD.inc(v.reduce_sum(stv("s1", par), xa, axis=AX))
                    v.wait_ge(s_lnD.sem, base + 1)
                    s_lnD.inc(v.tensor_scalar_mul(stv("mean", par), stv("s1", par), 1.0 / H))
                    v.wait_ge(s_lnD.sem, base + 2)
                    s_lnD.inc(v.tensor_mul(stv("m2", par), stv("mean", par), stv("mean", par)))
                    v.wait_ge(s_lnA.sem, ACT_IDX[("sq", ti)])
                    v.wait_ge(s_lnD.sem, base + 3)
                    s_lnD.inc(v.scalar_tensor_tensor(
                        stv("var", par), stv("ss", par), 1.0 / H, stv("m2", par),
                        op0=OP.mult, op1=OP.subtract))
                    v.wait_ge(s_lnA.sem, ACT_IDX[("sqrt", ti)])
                    s_lnD.inc(v.reciprocal(stv("rstd", par), stv("r0", par)))
                    v.wait_ge(s_lnD.sem, base + 2)
                    s_lnD.inc(v.tensor_scalar_mul(stv("nmr", par), stv("mean", par), -1.0))
                    v.wait_ge(s_lnD.sem, base + 6)
                    if ti >= 2:
                        v.wait_ge(s_xnw[ti % 2].sem, cnt2(ti - 2))
                    s_lnD.inc(v.tensor_scalar(
                        out=xnb[:, par * H:(par + 1) * H], in0=xa,
                        scalar1=stv("nmr", par), scalar2=stv("rstd", par),
                        op0=OP.add, op1=OP.mult))

            @block.scalar
            def _(s):
                def evacs(i):
                    for h in range(NKV):
                        n = i * NKV + h
                        s.wait_ge(s_pe.sem, n + 1)
                        if n >= 2:
                            s.wait_ge(s_kvw[n % 2].sem, cnt2(n - 2))
                        s_ev.inc(s.activation(
                            out=kvev[:, (n % 2) * TW:(n % 2 + 1) * TW],
                            in_=kpsum[:, h * TW:(h + 1) * TW],
                            func=AF.Copy))

                for pos0, (kind, ti) in enumerate(act_sched):
                    par = ti % 2
                    xa = xbuf[:, par * H:(par + 1) * H]
                    if kind == "sq":
                        s.wait_ge(s_x[ti % 2].sem, cnt2(ti))
                        if ti >= 1:
                            s.wait_ge(s_lnA.sem, ACT_IDX[("sq", ti - 1)])
                        if ti >= 2:
                            s.wait_ge(s_lnD.sem, (ti - 2) * LND + 4)
                        s_lnA.inc(s.activation(out=sqs[:], in_=xa, func=AF.Square,
                                               accum_out=stv("ss", par)))
                    else:
                        if ti == 0:
                            s.wait_ge(s_gp.sem, SETUP_GP)
                        s.wait_ge(s_lnD.sem, ti * LND + 4)
                        s_lnA.inc(s.activation(out=stv("r0", par), in_=stv("var", par),
                                               func=AF.Sqrt, bias=epscol[:], scale=1.0))
                        if ti % 4 == 1 and ti >= 9:
                            evacs(ti // 4 - 2)
                evacs(NT - 2)
                evacs(NT - 1)
            @block.gpsimd
            def _(g):
                def xload(ti):
                    s_x[ti % 2].inc(g.dma_start(
                        out=xbuf[:, (ti % 2) * H:(ti % 2 + 1) * H],
                        in_=x_all[ti * 128:(ti + 1) * 128, :]))

                def ktw(i):
                    for h in range(NKV):
                        n = i * NKV + h
                        g.wait_ge(s_ev.sem, n + 1)
                        s_kvw[n % 2].inc(g.dma_start(
                            out=kt_d.ap()[h, :, i * TW:(i + 1) * TW],
                            in_=kvev[:, (n % 2) * TW:(n % 2 + 1) * TW]))

                xload(0)
                xload(1)
                for part in range(8):
                    s_wk.inc(g.dma_start(
                        out=wkb[:, part * 4096:(part + 1) * 4096].rearrange(
                            "p (c n) -> p c n", c=4),
                        in_=w_k[part * 512:(part + 1) * 512, :].rearrange(
                            "(c p) n -> p c n", p=128)))
                for ti in range(NTILE):
                    par = ti % 2
                    g.wait_ge(s_lnD.sem, ti * LND + 7)
                    g.wait_ge(s_lnA.sem, ACT_IDX[("sq", ti)])
                    if ti + 2 < NTILE:
                        xload(ti + 2)
                    s_xnw[par].inc(g.dma_start(
                        out=xn_tok.ap()[ti * 128:(ti + 1) * 128, :],
                        in_=xnb[:, par * H:(par + 1) * H]))
                    if ti % 4 == 1 and ti >= 9:
                        ktw(ti // 4 - 2)
                ktw(NT - 2)
                ktw(NT - 1)

            @block.tensor
            def _(t):
                for i in range(NT):
                    t.wait_ge(s_xnt[i % 2].sem, xr_cnt(i))
                    if i == 0:
                        t.wait_ge(s_wk.sem, 16 * 8)
                    par = i % 2
                    for h in range(NKV):
                        n = i * NKV + h
                        if i >= 1:
                            t.wait_ge(s_ev.sem, (i - 1) * NKV + h + 1)
                        for hc in range(32):
                            ins = t.matmul(
                                kpsum[:, h * TW:(h + 1) * TW],
                                wkb[:, (hc * NKV + h) * D:(hc * NKV + h + 1) * D],
                                xnt[:, (par * 32 + hc) * TW:(par * 32 + hc + 1) * TW],
                                start=(hc == 0), stop=(hc == 31))
                        s_pe.inc(ins)

        SNAP1 = snap_ctrs()
        K_PE = NT * NKV
        K_EV = K_PE
        K_KVW = K_PE

        if stage >= 2:
            with ExitStack() as ph:
                psb = lambda n, s, d: ph.enter_context(nc.sbuf_tensor(n, s, d))
                wvb = psb("wvb", [128, 32 * NKV * D], BF16)
                xnt = psb("xnt_v", [128, 2 * 32 * TW], BF16)
                kvev = psb("kvev_v", [128, 2 * TW], BF16)
                kpsum = ph.enter_context(nc.psum_tensor("vpsum", [128, 2 * TW], F32))

                def tread_v(eng, i, h0, h1):
                    if i >= 2:
                        eng.wait_ge(s_pe.sem, K_PE + (i - 1) * 8)
                    par = i % 2
                    for hc in range(h0, h1):
                        s_xnt[par].inc(eng.dma_start_transpose(
                            xnt[:, (par * 32 + hc) * TW:(par * 32 + hc + 1) * TW],
                            xn_tok.ap()[i * TW:(i + 1) * TW, hc * 128:(hc + 1) * 128]))

                @block.gpsimd
                def _(g):
                    fence(g, SNAP1)
                    for part in range(8):
                        s_wv.inc(g.dma_start(
                            out=wvb[:, part * 4096:(part + 1) * 4096].rearrange(
                                "p (c n) -> p c n", c=4),
                            in_=w_v[part * 512:(part + 1) * 512, :].rearrange(
                                "(c p) n -> p c n", p=128)))
                    with g.register("r_ta") as r_ta, g.register("r_tb") as r_tb:
                        g.reg_load(r_ta, itb[0:1, 10:11])
                        g.reg_load(r_tb, itb[0:1, 11:12])
                        ta = g.snap(r_ta, min_val=0, max_val=S - TW)
                        tb = g.snap(r_tb, min_val=0, max_val=S - TW)
                        s_bnc.inc(g.dma_start(out=xn_own_d.ap()[0:TW, :],
                                              in_=xn_tok.ap()[bass.ds(ta, TW), :]))
                        s_bnc.inc(g.dma_start(out=xn_own_d.ap()[TW:OWN, :],
                                              in_=xn_tok.ap()[bass.ds(tb, TW), :]))
                    for i in range(NT):
                        for ts in range(4):
                            for grp in range(2):
                                n = K_EV + i * 8 + ts * 2 + grp
                                g.wait_ge(s_ev.sem, n + 1)
                                s_kvw[n % 2].inc(g.dma_start(
                                    out=v_d.ap()[i * 4 + ts, :, grp * TW:(grp + 1) * TW],
                                    in_=kvev[:, (n % 2) * TW:(n % 2 + 1) * TW]))

                @block.sync
                def _(sync):
                    fence(sync, SNAP1)
                    for i in range(NT):
                        tread_v(sync, i, 0, 32)

                @block.tensor
                def _(t):
                    fence(t, SNAP1)
                    t.wait_ge(s_wv.sem, 16 * 8)
                    for i in range(NT):
                        t.wait_ge(s_xnt[i % 2].sem, xr_cnt(NT + i))
                        par = i % 2
                        for ts in range(4):
                            for grp in range(2):
                                n = K_PE + i * 8 + ts * 2 + grp
                                if n >= 2:
                                    t.wait_ge(s_ev.sem, n - 1)
                                for hc in range(32):
                                    xoff = (par * 32 + hc) * TW + ts * 128
                                    ins = t.matmul(
                                        kpsum[:, (n % 2) * TW:(n % 2 + 1) * TW],
                                        xnt[:, xoff:xoff + 128],
                                        wvb[:, (hc * NKV + grp * 4) * D:
                                            (hc * NKV + grp * 4 + 4) * D],
                                        start=(hc == 0), stop=(hc == 31))
                                s_pe.inc(ins)

                @block.scalar
                def _(s):
                    def evacs_v(i):
                        for ts in range(4):
                            for grp in range(2):
                                n = K_EV + i * 8 + ts * 2 + grp
                                s.wait_ge(s_pe.sem, n + 1)
                                if n >= 2:
                                    s.wait_ge(s_kvw[n % 2].sem, cnt2(n - 2))
                                s_ev.inc(s.activation(
                                    out=kvev[:, (n % 2) * TW:(n % 2 + 1) * TW],
                                    in_=kpsum[:, (n % 2) * TW:(n % 2 + 1) * TW],
                                    func=AF.Copy))
                    fence(s, SNAP1)
                    for i in range(NT):
                        evacs_v(i)

        SNAP2 = snap_ctrs()
        V_PE = K_PE + NT * 8
        V_EV = V_PE
        V_KVW = V_PE

        kt_s = nc.dram_tensor("kt_s", [NKV, 128, NCHUNK * 128], BF16, kind="Internal")
        v_s = nc.dram_tensor("v_s", [NCHUNK, 128, 2 * TW], BF16, kind="Internal")
        q_s = nc.dram_tensor("q_s", [NQH, 128, NSLOT * TW], BF16, kind="Internal")
        s_scp = [SEM(f"s_scp{i}", 16) for i in range(NSLOT)]
        SYMS = {}
        xnto = None
        if stage >= 2:
            xnto = ctx.enter_context(nc.sbuf_tensor("xnto", [128, 32 * OWN], BF16))
            with ExitStack() as ph:
                psb = lambda n, s, d: ph.enter_context(nc.sbuf_tensor(n, s, d))
                wqb = psb("wqb", [128, 2 * 32 * 8 * D], BF16)
                kvev = psb("kvev_q", [128, 2 * TW], BF16)
                kpsum = ph.enter_context(nc.psum_tensor("qpsum", [128, 2 * TW], F32))

                def tread_q(eng, t2, h0, h1):
                    eng.wait_ge(s_bnc.sem, 32)
                    for hc in range(h0, h1):
                        s_xnt[t2 % 2].inc(eng.dma_start_transpose(
                            xnto[:, hc * OWN + t2 * TW:hc * OWN + (t2 + 1) * TW],
                            xn_own_d.ap()[t2 * TW:(t2 + 1) * TW,
                                          hc * 128:(hc + 1) * 128]))

                @block.gpsimd
                def _(g):
                    fence(g, SNAP2)
                    def qwrites(qq):
                        for t2 in range(2):
                            for qh8 in range(8):
                                n = V_EV + qq * 16 + t2 * 8 + qh8
                                qh = qq * 8 + qh8
                                g.wait_ge(s_ev.sem, n + 1)
                                s_kvw[n % 2].inc(g.dma_start(
                                    out=q_d.ap()[qh, :, t2 * TW:(t2 + 1) * TW],
                                    in_=kvev[:, (n % 2) * TW:(n % 2 + 1) * TW]))

                    for qq in range(4):
                        if qq >= 2:
                            g.wait_ge(s_pe.sem, V_PE + (qq - 1) * 16)
                        for part in range(4):
                            s_wq[qq % 2].inc(g.dma_start(
                                out=wqb[:, ((qq % 2) * 32 + part * 8) * 1024:
                                        ((qq % 2) * 32 + part * 8 + 8) * 1024].rearrange(
                                    "p (c n) -> p c n", c=8),
                                in_=w_q[part * 1024:(part + 1) * 1024,
                                        qq * 1024:(qq + 1) * 1024].rearrange(
                                    "(c p) n -> p c n", p=128)))
                        if qq >= 1:
                            qwrites(qq - 1)
                    qwrites(3)

                @block.sync
                def _(sync):
                    fence(sync, SNAP2)
                    sync.wait_ge(s_itb.sem, 32)
                    for t2 in range(2):
                        tread_q(sync, t2, 0, 32)
                    rk = [sync.alloc_register(f"rkb{i}") for i in range(NSLOT)]
                    rq = [sync.alloc_register(f"rqo{i}") for i in range(NSLOT)]
                    kbs, kb128, qos = [], [], []
                    for i in range(NSLOT):
                        sync.reg_load(rk[i], itb[0:1, i:i + 1])
                        sync.reg_load(rq[i], itb[0:1, 5 + i:6 + i])
                        kbs.append(sync.snap(rk[i], min_val=0,
                                             max_val=64 - SLOT_CHUNKS[i]))
                        kb128.append(sync.snap(kbs[i] * 128, min_val=0,
                                               max_val=(64 - SLOT_CHUNKS[i]) * 128))
                        qos.append(sync.snap(rq[i], min_val=0, max_val=TW))
                    SYMS["qos"] = qos
                    spares = [sync.alloc_register(f"spare{i}") for i in range(16)]
                    first = [True]

                    def sym_dma(*a, **kw):
                        ins = sync.dma_start(*a, **kw)
                        if first[0]:
                            for sp in spares:
                                sync.free_register(sp)
                            first[0] = False
                        return ins

                    for sl in range(NSLOT):
                        cnt = SLOT_CHUNKS[sl]
                        s_scp[sl].inc(sym_dma(
                            out=kt_s.ap()[:, :, SLOTK_OFF[sl] * 128:
                                          (SLOTK_OFF[sl] + cnt) * 128],
                            in_=kt_d.ap()[:, :, bass.ds(kb128[sl], cnt * 128)]))
                        s_scp[sl].inc(sym_dma(
                            out=v_s.ap()[SLOTK_OFF[sl]:SLOTK_OFF[sl] + cnt, :, :],
                            in_=v_d.ap()[bass.ds(kbs[sl], cnt), :, :]))

                @block.tensor
                def _(t):
                    fence(t, SNAP2)
                    for qq in range(4):
                        t.wait_ge(s_wq[qq % 2].sem, 16 * 4 * (qq // 2 + 1))
                        for t2 in range(2):
                            if qq == 0:
                                t.wait_ge(s_xnt[t2].sem, xr_cnt(2 * NT + t2))
                            for qh8 in range(8):
                                n = V_PE + qq * 16 + t2 * 8 + qh8
                                if n >= 2:
                                    t.wait_ge(s_ev.sem, n - 1)
                                for hc in range(32):
                                    ins = t.matmul(
                                        kpsum[:, (n % 2) * TW:(n % 2 + 1) * TW],
                                        wqb[:, (((qq % 2) * 32 + hc) * 8 + qh8) * D:
                                            (((qq % 2) * 32 + hc) * 8 + qh8 + 1) * D],
                                        xnto[:, hc * OWN + t2 * TW:hc * OWN + (t2 + 1) * TW],
                                        start=(hc == 0), stop=(hc == 31))
                                s_pe.inc(ins)

                @block.scalar
                def _(s):
                    fence(s, SNAP2)
                    for qq in range(4):
                        for t2 in range(2):
                            for qh8 in range(8):
                                n = V_EV + qq * 16 + t2 * 8 + qh8
                                s.wait_ge(s_pe.sem, n + 1)
                                if n >= 2:
                                    s.wait_ge(s_kvw[n % 2].sem, cnt2(n - 2))
                                s_ev.inc(s.activation(
                                    out=kvev[:, (n % 2) * TW:(n % 2 + 1) * TW],
                                    in_=kpsum[:, (n % 2) * TW:(n % 2 + 1) * TW],
                                    func=AF.Copy, scale=QSCALE))

        Q_PE = V_PE + 64
        Q_EV = Q_PE
        Q_KVW = Q_PE

        if stage < 3:
            @block.gpsimd
            def _(g):
                if stage >= 1:
                    g.wait_ge(s_xnw[0].sem, cnt2(NTILE - 2))
                    g.wait_ge(s_xnw[1].sem, cnt2(NTILE - 1))
                    s_out.inc(g.dma_start(out=dbg[0:128, 0:H],
                                          in_=xn_tok.ap()[256:384, :]))
                if stage >= 2:
                    g.wait_ge(s_kvw[0].sem, cnt2(Q_KVW - 2))
                    g.wait_ge(s_kvw[1].sem, cnt2(Q_KVW - 1))
                    s_out.inc(g.dma_start(out=dbg[0:128, H:H + 4096],
                                          in_=kt_d.ap()[2, :, 0:4096]))
                    s_out.inc(g.dma_start(out=dbg[0:128, 2 * H:2 * H + 1024],
                                          in_=v_d.ap()[7, :, :]))
                    s_out.inc(g.dma_start(out=dbg[0:128, 3 * H:3 * H + 1024],
                                          in_=q_d.ap()[5, :, :]))
                s_out.inc(g.dma_start(out=out_own[0:128, :],
                                      in_=xn_tok.ap()[0:128, :]))
                g.wait_ge(s_out.sem, s_out.n)
            return nc

        SNAP3 = snap_ctrs()

        pt_d = nc.dram_tensor("pt_d", [NQH, 128, OWN], BF16, kind="Internal")

        jobs = []
        for kv in range(NKV):
            for g_ in range(G):
                for sl in range(NSLOT):
                    jobs.append((kv, g_, sl))
        NJOB = len(jobs)
        NBAT = [SLOT_CHUNKS[sl] // 2 for (_, _, sl) in jobs]
        NBcum = [0]
        for b_ in NBAT:
            NBcum.append(NBcum[-1] + b_)

        s_ktl = [SEM("s_ktl0", 16), SEM("s_ktl1", 16)]
        s_vl = [SEM("s_vl0", 16), SEM("s_vl1", 16)]
        s_ql = [SEM("s_ql0", 16), SEM("s_ql1", 16)]
        s_el = [SEM("s_el0", 16), SEM("s_el1", 16)]
        s_sco = SEM("s_sco")
        s_exp = SEM("s_exp")
        s_mul = SEM("s_mul")
        s_pv = SEM("s_pv")
        s_drn = SEM("s_drn")
        s_fin = SEM("s_fin")
        s_rcp = SEM("s_rcp")
        s_pnv = SEM("s_pnv")
        s_tpx = SEM("s_tpx")
        s_tpe = SEM("s_tpe")
        s_ptw = [SEM("s_ptw0", 16), SEM("s_ptw1", 16)]
        s_p2i = SEM("s_p2i")

        with ExitStack() as ph:
            psb = lambda n, s, d: ph.enter_context(nc.sbuf_tensor(n, s, d))
            ktb = psb("ktb", [128, 2 * KMAX * 128], BF16)
            vab = psb("vab", [128, 2 * KMAX * 132], BF16)
            qsb = psb("qsb", [128, 2 * TW], BF16)
            esb = psb("esb", [128, 2 * EMAX], F32)
            ubuf = psb("ubuf", [128, 2 * 1024], F32)
            pbb = psb("pbb", [128, 4 * 1024], BF16)
            acc = psb("acc", [128, NSLOT * 520], F32)
            tfb = psb("tfb", [128, 2 * 520], F32)
            rcpb = psb("rcpb", [128, 8], F32)
            m20 = psb("m20", [128, 1], F32)
            pnv = psb("pnv", [128, 2 * 4096], BF16)
            ptb = psb("ptb", [128, 2 * 4096], BF16)
            spsum = ph.enter_context(nc.psum_tensor("spsum", [128, 2048], F32))
            pvps = ph.enter_context(nc.psum_tensor("pvps", [128, 1024], F32))
            tpp = [ph.enter_context(nc.psum_tensor("tpp0", [128, 128], BF16)),
                   ph.enter_context(nc.psum_tensor("tpp1", [128, 128], BF16))]

            def eview(pj, sl, b):
                cnt = SLOT_CHUNKS[sl]
                off = pj * EMAX + (cnt - 2 - 2 * b) * 128
                base = esb[:, off:off + 512]
                return bass.AP(base.tensor, base.offset,
                               [list(base.ap[0]), [128, 2], [1, 512]])

            def _p2_transposes(t, kvd):
                t.wait_ge(s_pnv.sem, (kvd + 1) * 32)
                for j in range(32):
                    if j >= 2 or kvd >= 1:
                        t.wait_ge(s_tpe.sem, kvd * 32 + j - 1)
                    s_tpx.inc(t.transpose(
                        tpp[j % 2][:],
                        pnv[:, (kvd % 2) * 4096 + j * 128:
                            (kvd % 2) * 4096 + (j + 1) * 128],
                        ident[:]))

            def _p2_tevacs(v, kvd):
                if kvd >= 2:
                    v.wait_ge(s_ptw[kvd % 2].sem, 16 * (kvd // 2))
                for j in range(32):
                    v.wait_ge(s_tpx.sem, kvd * 32 + j + 1)
                    s_tpe.inc(v.tensor_copy(
                        out=ptb[:, (kvd % 2) * 4096 + j * 128:
                                (kvd % 2) * 4096 + (j + 1) * 128],
                        in_=tpp[j % 2][:]))

            @block.gpsimd
            def _(g):
                fence(g, SNAP3)
                for n, (kv, g_, sl) in enumerate(jobs):
                    pj = n % 2
                    qh = kv * G + g_
                    if n >= 2:
                        g.wait_ge(s_mul.sem, NBcum[n - 1])
                    s_el[pj].inc(g.dma_start(
                        out=esb[:, pj * EMAX:pj * EMAX + REG_W[sl]],
                        in_=etab[qh, :, REG_OFF[sl]:REG_OFF[sl] + REG_W[sl]]))

            @block.sync
            def _(sync):
                fence(sync, SNAP3)
                sync.wait_ge(s_kvw[0].sem, cnt2(Q_KVW - 2))
                sync.wait_ge(s_kvw[1].sem, cnt2(Q_KVW - 1))
                qos = SYMS["qos"]
                for sl in range(NSLOT):
                    s_scp[sl].inc(sync.dma_start(
                        out=q_s.ap()[:, :, sl * TW:(sl + 1) * TW],
                        in_=q_d.ap()[:, :, bass.ds(qos[sl], TW)]))

                def ptwrite(kv):
                    sync.wait_ge(s_tpe.sem, (kv + 1) * 32)
                    s_ptw[kv % 2].inc(sync.dma_start(
                        out=pt_d.ap()[kv * G:(kv + 1) * G, :, :]
                            .transpose([1, 0, 2]),
                        in_=ptb[:, (kv % 2) * 4096:(kv % 2 + 1) * 4096]
                            .rearrange("p (a b) -> p a b", b=OWN)))

                for n, (kv, g_, sl) in enumerate(jobs):
                    cnt = SLOT_CHUNKS[sl]
                    pj = n % 2
                    qh = kv * G + g_
                    voff = (kv // 4) * 512 + (kv % 4) * 128
                    cs = SLOTK_OFF[sl]
                    if n % 20 == 1 and n // 20 >= 2:
                        ptwrite(n // 20 - 2)
                    if n >= 2:
                        sync.wait_ge(s_pv.sem, NBcum[n - 1])
                    if n < 2 * NSLOT:
                        sync.wait_ge(s_scp[sl].sem, 48)
                    s_ktl[pj].inc(sync.dma_start(
                        out=ktb[:, pj * KMAX * 128:pj * KMAX * 128 + cnt * 128],
                        in_=kt_s.ap()[kv, :, cs * 128:(cs + cnt) * 128]))
                    s_vl[pj].inc(sync.dma_start(
                        out=vab[:, pj * KMAX * 132:(pj * KMAX + cnt) * 132]
                            .rearrange("p (a b) -> p a b", b=132)[:, :, 0:128],
                        in_=v_s.ap()[cs:cs + cnt, :, voff:voff + 128]
                            .transpose([1, 0, 2])))
                    s_ql[pj].inc(sync.dma_start(
                        out=qsb[:, pj * TW:(pj + 1) * TW],
                        in_=q_s.ap()[qh, :, sl * TW:(sl + 1) * TW]))
                ptwrite(NKV - 2)
                ptwrite(NKV - 1)

            BATCHES = []
            for n, (kv, g_, sl) in enumerate(jobs):
                for b in range(SLOT_CHUNKS[sl] // 2):
                    BATCHES.append((n, b))

            @block.tensor
            def _(t):
                fence(t, SNAP3)
                t.wait_ge(s_p2i.sem, 2)

                def scores(nb):
                    n, b = BATCHES[nb]
                    _, _, sl = jobs[n]
                    pj = n % 2
                    pb = nb % 2
                    kbase = pj * KMAX * 128
                    if b == 0:
                        t.wait_ge(s_ktl[pj].sem, cnt2(n))
                        t.wait_ge(s_ql[pj].sem, cnt2(n))
                    if nb >= 2:
                        t.wait_ge(s_mul.sem, nb - 1)
                    for ci in range(2):
                        kc = 2 * b + 1 - ci
                        ins = t.matmul(
                            spsum[:, pb * 1024 + ci * 512:
                                  pb * 1024 + (ci + 1) * 512],
                            ktb[:, kbase + kc * 128:kbase + (kc + 1) * 128],
                            qsb[:, pj * TW:(pj + 1) * TW],
                            start=True, stop=True)
                    s_sco.inc(ins)

                def pv(nb):
                    n, b = BATCHES[nb]
                    _, _, sl = jobs[n]
                    B = SLOT_CHUNKS[sl] // 2
                    pj = n % 2
                    pb = nb % 4
                    if b == 0:
                        t.wait_ge(s_vl[pj].sem, cnt2(n))
                        t.wait_ge(s_drn.sem, n)
                    t.wait_ge(s_exp.sem, nb + 1)
                    for ci in range(2):
                        kc = 2 * b + 1 - ci
                        for sub in range(4):
                            ins = t.matmul(
                                pvps[:, sub * 256:sub * 256 + 129],
                                pbb[:, pb * 1024 + ci * 512 + sub * 128:
                                    pb * 1024 + ci * 512 + (sub + 1) * 128],
                                vab[:, (pj * KMAX + kc) * 132:
                                    (pj * KMAX + kc) * 132 + 129],
                                start=(b == 0 and ci == 0 and sub in (0, 2)),
                                stop=(b == B - 1 and ci == 1 and sub in (1, 3)))
                    s_pv.inc(ins)

                def pv_boundary(nb):
                    pv(nb)
                    np_, bp = BATCHES[nb]
                    if np_ % 20 == 19 and bp == NBAT[np_] - 1 and np_ // 20 >= 1:
                        _p2_transposes(t, np_ // 20 - 1)

                NB = len(BATCHES)
                for nb in range(NB):
                    if nb >= 3:
                        pv_boundary(nb - 3)
                    scores(nb)
                pv_boundary(NB - 3)
                pv_boundary(NB - 2)
                pv_boundary(NB - 1)
                _p2_transposes(t, NKV - 1)

            @block.scalar
            def _(s):
                fence(s, SNAP3)
                s.wait_ge(s_p2i.sem, 2)
                nb = 0
                for n in range(NJOB):
                    for b in range(NBAT[n]):
                        pb = nb % 2
                        s.wait_ge(s_mul.sem, nb + 1)
                        if nb >= 4:
                            s.wait_ge(s_pv.sem, nb - 3)
                        s_exp.inc(s.activation(
                            out=pbb[:, (nb % 4) * 1024:(nb % 4 + 1) * 1024],
                            in_=ubuf[:, pb * 1024:(pb + 1) * 1024],
                            func=AF.Exp, bias=m20[:], scale=1.0))
                        nb += 1

            @block.vector
            def _(v):
                fence(v, SNAP3)
                s_p2i.inc(v.memset(m20[:], EXPB))
                s_p2i.inc(v.memset(
                    vab[:].rearrange("p (a b) -> p a b", b=132)[:, :, 128:129], 1.0))
                s_p2i.inc(v.memset(acc[:], 0.0))
                nb = 0
                for n, (kv, g_, sl) in enumerate(jobs):
                    cnt = SLOT_CHUNKS[sl]
                    B = cnt // 2
                    pj = n % 2
                    m = kv * G + g_
                    for b in range(B):
                        pb = nb % 2
                        v.wait_ge(s_sco.sem, nb + 1)
                        if b == 0:
                            v.wait_ge(s_el[pj].sem, cnt2(n))
                        if nb >= 2:
                            v.wait_ge(s_exp.sem, nb - 1)
                        s_mul.inc(v.tensor_tensor(
                            out=ubuf[:, pb * 1024:(pb + 1) * 1024]
                                .rearrange("p (a b) -> p a b", a=2),
                            in0=spsum[:, pb * 1024:(pb + 1) * 1024]
                                .rearrange("p (a b) -> p a b", a=2),
                            in1=eview(pj, sl, b), op=OP.add))
                        nb += 1
                    v.wait_ge(s_pv.sem, NBcum[n + 1])
                    if m >= 1:
                        v.wait_ge(s_fin.sem, m * 10)
                    s_drn.inc(v.tensor_copy(
                        out=acc[:, sl * 520:(sl + 1) * 520]
                            .rearrange("p (a b) -> p a b", b=130)[:, :, 0:129],
                        in_=pvps[:].rearrange("p (a b) -> p a b", b=256)[:, :, 0:129]))
                    if sl == NSLOT - 1:
                        v.wait_ge(s_drn.sem, 5 * m + 5)
                        if m >= 1:
                            v.wait_ge(s_pnv.sem, m * 8)
                            v.wait_ge(s_rcp.sem, m * 2)
                        if m == 0:
                            v.wait_ge(s_itb.sem, 32)
                        for tile in range(2):
                            base2 = m * 10 + tile * 5
                            s_fin.inc(v.tensor_scalar(
                                out=tfb[:, tile * 520:(tile + 1) * 520],
                                in0=acc[:, 0:520],
                                scalar1=ftb[:, tile * NSLOT:tile * NSLOT + 1],
                                scalar2=None, op0=OP.mult))
                            for sl2 in range(1, NSLOT):
                                v.wait_ge(s_fin.sem, base2 + sl2)
                                s_fin.inc(v.scalar_tensor_tensor(
                                    out=tfb[:, tile * 520:(tile + 1) * 520],
                                    in0=acc[:, sl2 * 520:(sl2 + 1) * 520],
                                    scalar=ftb[:, tile * NSLOT + sl2:
                                               tile * NSLOT + sl2 + 1],
                                    in1=tfb[:, tile * 520:(tile + 1) * 520],
                                    op0=OP.mult, op1=OP.add))
                        v.wait_ge(s_fin.sem, (m + 1) * 10)
                        for tile in range(2):
                            s_rcp.inc(v.reciprocal(
                                rcpb[:, tile * 4:(tile + 1) * 4]
                                    .rearrange("p (a b) -> p a b", b=1),
                                tfb[:, tile * 520:(tile + 1) * 520]
                                    .rearrange("p (a b) -> p a b", b=130)
                                    [:, :, 128:129]))
                        if kv >= 2 and g_ == 0:
                            v.wait_ge(s_tpx.sem, (kv - 1) * 32)
                        for tile in range(2):
                            for sub in range(4):
                                v.wait_ge(s_rcp.sem, m * 2 + tile + 1)
                                s_pnv.inc(v.tensor_scalar(
                                    out=pnv[:, (kv % 2) * 4096 +
                                            (g_ * 8 + tile * 4 + sub) * 128:
                                            (kv % 2) * 4096 +
                                            (g_ * 8 + tile * 4 + sub + 1) * 128],
                                    in0=tfb[:, tile * 520 + sub * 130:
                                            tile * 520 + sub * 130 + 128],
                                    scalar1=rcpb[:, tile * 4 + sub:
                                                 tile * 4 + sub + 1],
                                    scalar2=None, op0=OP.mult))
                    if n % 20 == 19 and n // 20 >= 1:
                        _p2_tevacs(v, n // 20 - 1)
                _p2_tevacs(v, NKV - 1)


        SNAP4 = snap_ctrs()

        if stage == 3:
            @block.gpsimd
            def _(g):
                g.wait_ge(s_ptw[0].sem, 16 * 4)
                g.wait_ge(s_ptw[1].sem, 16 * 4)
                s_out.inc(g.dma_start(out=dbg[0:128, 0:1024],
                                      in_=pt_d.ap()[2, :, :]))
                s_out.inc(g.dma_start(out=dbg[0:128, 1024:2048],
                                      in_=pt_d.ap()[17, :, :]))
                s_out.inc(g.dma_start(out=out_own[0:128, :],
                                      in_=xn_tok.ap()[0:128, :]))
                g.wait_ge(s_out.sem, s_out.n)
            return nc

        s_p3w = SEM("s_p3w", 16)
        s_p3i = SEM("s_p3i")
        s_gz = SEM("s_gz")
        s_p3d = SEM("s_p3d")
        s_ztx = SEM("s_ztx")
        s_zte = SEM("s_zte")
        s_ptl = SEM("s_ptl", 16)
        s_wo2 = [SEM("s_wo0", 16), SEM("s_wo1", 16)]
        s_p3g = SEM("s_p3g")
        s_p3pe = SEM("s_p3pe")
        s_epi = SEM("s_epi")
        s_xr = [SEM("s_xr0", 16), SEM("s_xr1", 16)]
        s_ost = [SEM("s_ost0", 16), SEM("s_ost1", 16)]
        P3D = 7

        with ExitStack() as ph:
            psb = lambda n, s, d: ph.enter_context(nc.sbuf_tensor(n, s, d))
            pts = psb("pts", [128, 32 * OWN], BF16)
            gwb = psb("gwb", [128, 128], BF16)
            lab = psb("lab", [128, 1024], BF16)
            gbb = psb("gbb", [1, 4], BF16)
            ones1 = psb("ones1", [1, 128], BF16)
            lbb = [psb(f"lbb{e}", [8, H], BF16) for e in range(4)]
            gzf = psb("gzf", [128, 40], F32)
            ohb = psb("ohb", [128, 5], F32)
            zmb = psb("zmb", [128, 8 * 32], BF16)
            ztb = [psb(f"ztb{e}", [8, 8 * 128], BF16) for e in range(4)]
            wob = psb("wob", [128, 2 * 4 * TW], BF16)
            wub = psb("wub", [128, 2 * 4 * TW], BF16)
            xrb = psb("xrb", [128, 2 * TW], F32)
            outb = psb("outb", [128, 2 * TW], F32)

            def _p3_ztrans(t, tsd):
                for e in range(4):
                    j = tsd * 4 + e
                    t.wait_ge(s_p3d.sem, tsd * P3D + P3D)
                    if j >= 2:
                        t.wait_ge(s_zte.sem, j - 1)
                    s_ztx.inc(t.transpose(
                        tpz[j % 2][:],
                        zmb[:, tsd * 32 + e * 8:tsd * 32 + (e + 1) * 8],
                        ident[:]))

            def _p3_ztevac(v, tsd):
                for e in range(4):
                    j = tsd * 4 + e
                    v.wait_ge(s_ztx.sem, j + 1)
                    s_zte.inc(v.tensor_copy(
                        out=ztb[e][:, tsd * 128:(tsd + 1) * 128],
                        in_=tpz[j % 2][:]))

            with ExitStack() as ph2:
                psg = ph2.enter_context(nc.psum_tensor("psg", [128, 64], F32))
                tpz = [ph2.enter_context(nc.psum_tensor("tpz0", [8, 128], BF16)),
                       ph2.enter_context(nc.psum_tensor("tpz1", [8, 128], BF16))]

                @block.gpsimd
                def _(g):
                    fence(g, SNAP4)
                    s_p3w.inc(g.dma_start(out=gwb[:], in_=gate_wt))
                    s_p3w.inc(g.dma_start(out=lab[:], in_=lora_at))
                    s_p3w.inc(g.dma_start(out=gbb[:], in_=gate_b))
                    for e in range(4):
                        s_p3w.inc(g.dma_start(out=lbb[e][:],
                                              in_=lora_bt[e * 8:(e + 1) * 8, :]))
                    s_p3i.inc(g.memset(ones1[:], 1.0))

                @block.sync
                def _(sync):
                    fence(sync, SNAP4)
                    sync.wait_ge(s_ptw[0].sem, 16 * 4)
                    sync.wait_ge(s_ptw[1].sem, 16 * 4)
                    for kv8 in range(4):
                        s_ptl.inc(sync.dma_start(
                            out=pts[:, kv8 * 8 * OWN:(kv8 + 1) * 8 * OWN]
                                .rearrange("p (a b) -> p a b", b=OWN),
                            in_=pt_d.ap()[kv8 * 8:(kv8 + 1) * 8, :, :]
                                .transpose([1, 0, 2])))

                @block.tensor
                def _(t):
                    fence(t, SNAP4)
                    t.wait_ge(s_p3w.sem, 16 * 7)
                    t.wait_ge(s_p3i.sem, 1)
                    for ts in range(8):
                        if ts >= 1:
                            t.wait_ge(s_p3d.sem, (ts - 1) * P3D + 1)
                        for hc in range(32):
                            t.matmul(psg[:, 0:4],
                                     xnto[:, hc * OWN + ts * 128:
                                          hc * OWN + (ts + 1) * 128],
                                     gwb[:, hc * 4:(hc + 1) * 4],
                                     start=(hc == 0), stop=False)
                        t.matmul(psg[:, 0:4],
                                 ones1[:], gbb[:], start=False, stop=True)
                        for hc in range(32):
                            ins = t.matmul(
                                psg[:, 4:36],
                                xnto[:, hc * OWN + ts * 128:
                                     hc * OWN + (ts + 1) * 128],
                                lab[:, hc * 32:(hc + 1) * 32],
                                start=(hc == 0), stop=(hc == 31))
                        s_gz.inc(ins)
                        if ts >= 1:
                            _p3_ztrans(t, ts - 1)
                    _p3_ztrans(t, 7)

                @block.vector
                def _(v):
                    fence(v, SNAP4)
                    for ts in range(8):
                        base = ts * P3D
                        v.wait_ge(s_gz.sem, ts + 1)
                        if ts >= 1:
                            v.wait_ge(s_p3d.sem, base)
                        s_p3d.inc(v.tensor_copy(gzf[:, 0:36], psg[:, 0:36]))
                        v.wait_ge(s_p3d.sem, base + 1)
                        s_p3d.inc(v.reduce_max(ohb[:, 4:5], gzf[:, 0:4], axis=AX))
                        v.wait_ge(s_p3d.sem, base + 2)
                        s_p3d.inc(v.tensor_scalar(
                            out=ohb[:, 0:4], in0=gzf[:, 0:4],
                            scalar1=ohb[:, 4:5], scalar2=None, op0=OP.is_equal))
                        for e in range(4):
                            v.wait_ge(s_p3d.sem, base + 3)
                            s_p3d.inc(v.tensor_scalar(
                                out=zmb[:, ts * 32 + e * 8:ts * 32 + (e + 1) * 8],
                                in0=gzf[:, 4 + e * 8:4 + (e + 1) * 8],
                                scalar1=ohb[:, e:e + 1], scalar2=None, op0=OP.mult))
                        if ts >= 1:
                            _p3_ztevac(v, ts - 1)
                    _p3_ztevac(v, 7)

            with ExitStack() as ph3:
                pso = ph3.enter_context(nc.psum_tensor("pso", [128, 8 * TW], F32))

                @block.gpsimd
                def _(g):
                    for fs in range(8):
                        for hcg in range(8):
                            gw = fs * 8 + hcg
                            if gw >= 2:
                                g.wait_ge(s_p3g.sem, gw - 1)
                            for w_i, (wt, wbuf) in enumerate(((w_o, wob),
                                                              (w_up, wub))):
                                s_wo2[gw % 2].inc(g.dma_start(
                                    out=wbuf[:, ((gw % 2) * 4) * TW:
                                             ((gw % 2) * 4 + 4) * TW]
                                        .rearrange("p (c n) -> p c n", c=4),
                                    in_=wt[hcg * 512:(hcg + 1) * 512,
                                           fs * TW:(fs + 1) * TW]
                                        .rearrange("(c p) n -> p c n", p=128)))

                @block.sync
                def _(sync):
                    for k in range(64):
                        fs, ts = k // 8, k % 8
                        if k >= 2:
                            sync.wait_ge(s_epi.sem, k - 1)
                        s_xr[k % 2].inc(sync.dma_start(
                            out=xrb[:, (k % 2) * TW:(k % 2 + 1) * TW],
                            in_=x_own[ts * 128:(ts + 1) * 128,
                                      fs * TW:(fs + 1) * TW]))
                        if k >= 2:
                            k2 = k - 2
                            sync.wait_ge(s_epi.sem, k2 + 1)
                            s_ost[k2 % 2].inc(sync.dma_start(
                                out=out_own[(k2 % 8) * 128:(k2 % 8 + 1) * 128,
                                            (k2 // 8) * TW:(k2 // 8 + 1) * TW],
                                in_=outb[:, (k2 % 2) * TW:(k2 % 2 + 1) * TW]))
                    for k2 in (62, 63):
                        sync.wait_ge(s_epi.sem, k2 + 1)
                        s_ost[k2 % 2].inc(sync.dma_start(
                            out=out_own[(k2 % 8) * 128:(k2 % 8 + 1) * 128,
                                        (k2 // 8) * TW:(k2 // 8 + 1) * TW],
                            in_=outb[:, (k2 % 2) * TW:(k2 % 2 + 1) * TW]))
                    sync.wait_ge(s_ost[0].sem, 16 * 32)
                    sync.wait_ge(s_ost[1].sem, 16 * 32)

                @block.tensor
                def _(t):
                    t.wait_ge(s_ptl.sem, 16 * 4)
                    t.wait_ge(s_zte.sem, 32)
                    t.wait_ge(s_p3d.sem, 7 * P3D + 1)
                    for fs in range(8):
                        for hcg in range(8):
                            gw = fs * 8 + hcg
                            t.wait_ge(s_wo2[gw % 2].sem, 32 * (gw // 2 + 1))
                            last = None
                            for w_i, wbuf in enumerate((wob, wub)):
                                src = pts if w_i == 0 else xnto
                                for hc4 in range(4):
                                    hc = hcg * 4 + hc4
                                    for ts in range(8):
                                        if (hcg == 0 and w_i == 0 and hc4 == 0
                                                and fs >= 1):
                                            t.wait_ge(s_epi.sem,
                                                      (fs - 1) * 8 + ts + 1)
                                        last = t.matmul(
                                            pso[:, ts * TW:(ts + 1) * TW],
                                            src[:, hc * OWN + ts * 128:
                                                hc * OWN + (ts + 1) * 128],
                                            wbuf[:, ((gw % 2) * 4 + hc4) * TW:
                                                 ((gw % 2) * 4 + hc4 + 1) * TW],
                                            start=(hcg == 0 and w_i == 0
                                                   and hc4 == 0),
                                            stop=False)
                            s_p3g.inc(last)
                        for ts in range(8):
                            for e in range(4):
                                ins = t.matmul(
                                    pso[:, ts * TW:(ts + 1) * TW],
                                    ztb[e][:, ts * 128:(ts + 1) * 128],
                                    lbb[e][:, fs * TW:(fs + 1) * TW],
                                    start=False, stop=(e == 3))
                            s_p3pe.inc(ins)

                @block.vector
                def _(v):
                    for k in range(64):
                        fs, ts = k // 8, k % 8
                        v.wait_ge(s_p3pe.sem, k + 1)
                        v.wait_ge(s_xr[k % 2].sem, cnt2(k))
                        if k >= 2:
                            v.wait_ge(s_ost[k % 2].sem, cnt2(k - 2))
                        s_epi.inc(v.tensor_tensor(
                            out=outb[:, (k % 2) * TW:(k % 2 + 1) * TW],
                            in0=pso[:, ts * TW:(ts + 1) * TW],
                            in1=xrb[:, (k % 2) * TW:(k % 2 + 1) * TW],
                            op=OP.add))

    return nc



def _host_inputs(inputs, c):
    x = np.ascontiguousarray(inputs["L0_attn_hidden_states"].reshape(S, H),
                             dtype=np.float32)
    w_qkv = np.ascontiguousarray(inputs["L0_attn_w_qkv"], dtype=np.float32)
    tiles = (c, 15 - c)
    tile_of, kbase, mask = _plan(c)
    x_own = np.ascontiguousarray(
        np.concatenate([x[t * TW:(t + 1) * TW] for t in tiles], axis=0))
    gate_w = np.ascontiguousarray(inputs["L0_ffn_gate_w"], dtype=np.float32)
    gwt = np.ascontiguousarray(
        gate_w.T.reshape(32, 128, 4).transpose(1, 0, 2).reshape(128, 128))
    lora_A = np.ascontiguousarray(inputs["L0_ffn_lora_A"], dtype=np.float32)
    lat = np.ascontiguousarray(
        lora_A.reshape(32, H).T.reshape(32, 128, 32).transpose(1, 0, 2).reshape(128, 1024))
    lora_B = np.ascontiguousarray(inputs["L0_ffn_lora_B"], dtype=np.float32)
    lbt = np.ascontiguousarray(lora_B.transpose(0, 2, 1).reshape(32, H))
    it = np.zeros((1, 16), np.int32)
    it[0, 0:5] = kbase
    it[0, 5:10] = [tile_of[s] * TW for s in range(NSLOT)]
    it[0, 10] = tiles[0] * TW
    it[0, 11] = tiles[1] * TW
    ft = np.zeros((128, 16), np.float32)
    for t in range(2):
        for s in range(NSLOT):
            ft[:, t * NSLOT + s] = mask[t][s]
    return {
        "x_all": x,
        "x_own": x_own,
        "w_q": np.ascontiguousarray(w_qkv[:, 0:NQH * D]),
        "w_k": np.ascontiguousarray(w_qkv[:, NQH * D:NQH * D + NKV * D]),
        "w_v": np.ascontiguousarray(w_qkv[:, NQH * D + NKV * D:]),
        "w_o": np.ascontiguousarray(inputs["L0_attn_w_o"], dtype=np.float32),
        "w_up": np.ascontiguousarray(
            np.asarray(inputs["L0_ffn_W_up"], dtype=np.float32).T),
        "gate_wt": gwt,
        "gate_b": np.ascontiguousarray(
            inputs["L0_ffn_expert_bias"].reshape(1, 4), dtype=np.float32),
        "lora_at": lat,
        "lora_bt": lbt,
        "etab": _bake_etab(c),
        "itab": it,
        "ftab": ft,
    }


def kernel(**inputs):
    nc = build_nc(stage=4)
    in_maps = [_host_inputs(inputs, c) for c in range(NCORE)]
    res = run_bass_kernel_spmd(nc, in_maps, core_ids=list(range(NCORE)))
    out = np.empty((S, H), dtype=np.float32)
    for c in range(NCORE):
        o = res.results[c]["out_own"]
        out[c * TW:(c + 1) * TW] = o[0:TW]
        out[(15 - c) * TW:(16 - c) * TW] = o[TW:OWN]
    return out.reshape(1, S, H)


# version 11
# speedup vs baseline: 1.0752x; 1.0752x over previous
import math
import numpy as np
import ml_dtypes
from contextlib import ExitStack

import concourse.bass as bass
import concourse.mybir as mybir
from concourse.bass_utils import run_bass_kernel_spmd

F32 = mybir.dt.float32
BF16 = mybir.dt.bfloat16
I32 = mybir.dt.int32
AF = mybir.ActivationFunctionType
OP = mybir.AluOpType
AX = mybir.AxisListType.X

S = 8192
H = 4096
D = 128
NKV = 8
G = 4
NQH = 32
NT = 16
TW = 512
OWN = 2 * TW
EPS = 1e-5
QSCALE = 1.0 / math.sqrt(D)
EXPB = -20.0
NCORE = 8
NTILE = S // 128

SLOT_CHUNKS = [4, 8, 16, 16, 24]
NSLOT = 5
REG_W = [(c - 1) * 128 + TW for c in SLOT_CHUNKS]
REG_OFF = [0]
for _w in REG_W[:-1]:
    REG_OFF.append(REG_OFF[-1] + _w)
REG_TOT = REG_OFF[-1] + REG_W[-1]
EMAX = max(REG_W)
KMAX = max(SLOT_CHUNKS)

A_SETS = {0: [0], 1: [1], 2: [0, 1], 3: [2], 4: [0, 2], 5: [1, 2],
          6: [0, 1, 2], 7: [1, 4]}
SLOTK_OFF = [0]
for _c in SLOT_CHUNKS[:-1]:
    SLOTK_OFF.append(SLOTK_OFF[-1] + _c)
NCHUNK = sum(SLOT_CHUNKS)


def _alibi_slopes(n):
    cp2 = 2 ** math.floor(math.log2(n))
    base = 2.0 ** (-8.0 / cp2)
    slopes = base ** np.arange(1, cp2 + 1, dtype=np.float64)
    if cp2 != n:
        eb = 2.0 ** (-4.0 / cp2)
        extra = eb ** np.arange(1, 1 + 2 * (n - cp2), 2, dtype=np.float64)
        slopes = np.concatenate([slopes, extra])
    return slopes


def _plan(c):
    a_set = A_SETS[c]
    tile_of = [0 if s in a_set else 1 for s in range(NSLOT)]
    kbase = []
    cum = [0, 0]
    for s in range(NSLOT):
        t = tile_of[s]
        kbase.append(cum[t])
        cum[t] += SLOT_CHUNKS[s]
    assert cum[0] == (c + 1) * 4 and cum[1] == (16 - c) * 4
    mask = [[1.0 if tile_of[s] == t else 0.0 for s in range(NSLOT)]
            for t in range(2)]
    return tile_of, kbase, mask


def _bake_etab(c):
    slopes = _alibi_slopes(NQH).astype(np.float64)
    tile_of, kbase, _ = _plan(c)
    tiles = (c, 15 - c)
    out = np.zeros((NQH, 128, REG_TOT), dtype=np.float32)
    p = np.arange(128, dtype=np.int64)[:, None]
    for s in range(NSLOT):
        qb = tiles[tile_of[s]] * TW
        kb = kbase[s] * 128
        X = np.arange(REG_W[s], dtype=np.int64)[None, :]
        rel = qb + X - (SLOT_CHUNKS[s] - 1) * 128 - kb - p
        sq = np.sqrt(np.maximum(rel, 0).astype(np.float64))
        valid = rel >= 0
        for h in range(NQH):
            v = np.where(valid, -slopes[h] * sq, -1e9)
            out[h][:, REG_OFF[s]:REG_OFF[s] + REG_W[s]] = v.astype(np.float32)
    return out


class Ctr:
    def __init__(self, sem, step=1):
        self.sem = sem
        self.step = step
        self.n = 0

    def inc(self, ins):
        self.n += self.step
        return ins.then_inc(self.sem, self.step)


def build_nc(stage=4):
    nc = bass.Bass("TRN2", target_bir_lowering=False)

    x_all = nc.dram_tensor("x_all", [S, H], F32, kind="ExternalInput").ap()
    x_own = nc.dram_tensor("x_own", [OWN, H], F32, kind="ExternalInput").ap()
    w_q = nc.dram_tensor("w_q", [H, NQH * D], F32, kind="ExternalInput").ap()
    w_k = nc.dram_tensor("w_k", [H, NKV * D], F32, kind="ExternalInput").ap()
    w_v = nc.dram_tensor("w_v", [H, NKV * D], F32, kind="ExternalInput").ap()
    w_o = nc.dram_tensor("w_o", [H, H], F32, kind="ExternalInput").ap()
    w_up = nc.dram_tensor("w_up", [H, H], F32, kind="ExternalInput").ap()
    gate_wt = nc.dram_tensor("gate_wt", [128, 128], F32, kind="ExternalInput").ap()
    gate_b = nc.dram_tensor("gate_b", [1, 4], F32, kind="ExternalInput").ap()
    lora_at = nc.dram_tensor("lora_at", [128, 1024], F32, kind="ExternalInput").ap()
    lora_bt = nc.dram_tensor("lora_bt", [32, H], F32, kind="ExternalInput").ap()
    etab = nc.dram_tensor("etab", [NQH, 128, REG_TOT], F32, kind="ExternalInput").ap()
    itab = nc.dram_tensor("itab", [1, 16], I32, kind="ExternalInput").ap()
    ftab = nc.dram_tensor("ftab", [128, 16], F32, kind="ExternalInput").ap()

    out_own = nc.dram_tensor("out_own", [OWN, H], F32, kind="ExternalOutput").ap()
    dbg = None
    if stage < 4:
        dbg = nc.dram_tensor("dbg", [128, 4 * H], F32, kind="ExternalOutput").ap()

    xn_tok = nc.dram_tensor("xn_tok", [S, H], BF16, kind="Internal")
    xn_own_d = nc.dram_tensor("xn_own_d", [OWN, H], BF16, kind="Internal")
    kt_d = nc.dram_tensor("kt_d", [NKV, 128, S], BF16, kind="Internal")
    v_d = nc.dram_tensor("v_d", [S // 128, 128, 2 * TW], BF16, kind="Internal")
    q_d = nc.dram_tensor("q_d", [NQH, 128, OWN], BF16, kind="Internal")

    ctx = ExitStack()
    ALL_CTRS = []

    def SEM(n, step=1):
        c = Ctr(ctx.enter_context(nc.semaphore(n)), step)
        ALL_CTRS.append(c)
        return c

    with ctx:
        itb = ctx.enter_context(nc.sbuf_tensor("itb", [1, 16], I32))
        ftb = ctx.enter_context(nc.sbuf_tensor("ftb", [128, 16], F32))
        ident = ctx.enter_context(nc.sbuf_tensor("ident", [128, 128], BF16))
        zcol = ctx.enter_context(nc.sbuf_tensor("zcol", [128, 1], F32))
        epscol = ctx.enter_context(nc.sbuf_tensor("epscol", [128, 1], F32))

        s_itb = SEM("s_itb", 16)
        s_x = [SEM("s_x0", 16), SEM("s_x1", 16)]
        s_xnw = [SEM("s_xnw0", 16), SEM("s_xnw1", 16)]
        s_xnt = [SEM("s_xnt0", 16), SEM("s_xnt1", 16)]
        s_wk = SEM("s_wk", 16)
        s_wv = SEM("s_wv", 16)
        s_wq = [SEM("s_wq0", 16), SEM("s_wq1", 16)]
        s_lnA = SEM("s_lnA")
        s_lnD = SEM("s_lnD")
        s_pe = SEM("s_pe")
        s_ev = SEM("s_ev")
        s_kvw = [SEM("s_kvw0", 16), SEM("s_kvw1", 16)]
        s_bnc = SEM("s_bnc", 16)
        s_gp = SEM("s_gp")
        s_out = SEM("s_out", 16)

        block = ctx.enter_context(nc.Block())

        @block.sync
        def _(sync):
            sync.dma_start(out=itb[:], in_=itab).then_inc(s_itb.sem, 16)
            sync.dma_start(out=ftb[:], in_=ftab).then_inc(s_itb.sem, 16)
        s_itb.n = 32

        @block.gpsimd
        def _(g):
            s_gp.inc(g.memset(zcol[:], 0.0))
            s_gp.inc(g.memset(epscol[:], EPS))
            s_gp.inc(g.memset(ident[:], 0.0))
            g.wait_ge(s_gp.sem, 3)
            s_gp.inc(g.affine_select(
                out=ident[:], in_=ident[:], compare_op=OP.not_equal,
                fill=1.0, base=0, pattern=[[-1, 128]], channel_multiplier=1))
        SETUP_GP = 4

        cnt2 = lambda n: 16 * (n // 2 + 1)
        xr_cnt = lambda gi: 16 * 32 * (gi // 2 + 1)

        def snap_ctrs():
            return [(c, c.n) for c in ALL_CTRS]

        def fence(eng, snap):
            for c, v in snap:
                if v > 0:
                    eng.wait_ge(c.sem, v)

        act_sched = []
        for ti in range(NTILE):
            act_sched.append(("sq", ti))
            act_sched.append(("sqrt", ti))
        ACT_IDX = {}
        for pos, (kind, ti) in enumerate(act_sched):
            ACT_IDX[(kind, ti)] = pos + 1

        LND = 7

        with ExitStack() as ph:
            psb = lambda n, s, d: ph.enter_context(nc.sbuf_tensor(n, s, d))
            wkb = psb("wkb", [128, 32 * NKV * D], BF16)
            xbuf = psb("xbuf", [128, 2 * H], F32)
            xnb = psb("xnb", [128, 2 * H], BF16)
            sqs = psb("sqs", [128, H], F32)
            stt = psb("stt", [128, 16], F32)
            xnt = psb("xnt", [128, 2 * 32 * TW], BF16)
            kvev = psb("kvev", [128, 2 * TW], BF16)
            kpsum = ph.enter_context(nc.psum_tensor("kpsum", [128, 2 * TW], F32))

            def stv(name, par):
                i = {"s1": 0, "ss": 1, "mean": 2, "m2": 3, "var": 4,
                     "r0": 5, "rstd": 6, "nmr": 7}[name] * 2 + par
                return stt[:, i:i + 1]

            def tread(eng, i, h0, h1):
                eng.wait_ge(s_xnw[0].sem, cnt2(4 * i + 2))
                eng.wait_ge(s_xnw[1].sem, cnt2(4 * i + 3))
                if i >= 2:
                    eng.wait_ge(s_pe.sem, (i - 1) * NKV)
                par = i % 2
                for hc in range(h0, h1):
                    s_xnt[par].inc(eng.dma_start_transpose(
                        xnt[:, (par * 32 + hc) * TW:(par * 32 + hc + 1) * TW],
                        xn_tok.ap()[i * TW:(i + 1) * TW, hc * 128:(hc + 1) * 128]))

            @block.sync
            def _(sync):
                for i in range(NT):
                    tread(sync, i, 0, 32)

            @block.vector
            def _(v):
                for ti in range(NTILE):
                    par = ti % 2
                    xa = xbuf[:, par * H:(par + 1) * H]
                    base = ti * LND
                    v.wait_ge(s_x[ti % 2].sem, cnt2(ti))
                    if ti >= 2:
                        v.wait_ge(s_lnD.sem, (ti - 2) * LND + 2)
                    s_lnD.inc(v.reduce_sum(stv("s1", par), xa, axis=AX))
                    v.wait_ge(s_lnD.sem, base + 1)
                    s_lnD.inc(v.tensor_scalar_mul(stv("mean", par), stv("s1", par), 1.0 / H))
                    v.wait_ge(s_lnD.sem, base + 2)
                    s_lnD.inc(v.tensor_mul(stv("m2", par), stv("mean", par), stv("mean", par)))
                    v.wait_ge(s_lnA.sem, ACT_IDX[("sq", ti)])
                    v.wait_ge(s_lnD.sem, base + 3)
                    s_lnD.inc(v.scalar_tensor_tensor(
                        stv("var", par), stv("ss", par), 1.0 / H, stv("m2", par),
                        op0=OP.mult, op1=OP.subtract))
                    v.wait_ge(s_lnA.sem, ACT_IDX[("sqrt", ti)])
                    s_lnD.inc(v.reciprocal(stv("rstd", par), stv("r0", par)))
                    v.wait_ge(s_lnD.sem, base + 2)
                    s_lnD.inc(v.tensor_scalar_mul(stv("nmr", par), stv("mean", par), -1.0))
                    v.wait_ge(s_lnD.sem, base + 6)
                    if ti >= 2:
                        v.wait_ge(s_xnw[ti % 2].sem, cnt2(ti - 2))
                    s_lnD.inc(v.tensor_scalar(
                        out=xnb[:, par * H:(par + 1) * H], in0=xa,
                        scalar1=stv("nmr", par), scalar2=stv("rstd", par),
                        op0=OP.add, op1=OP.mult))

            @block.scalar
            def _(s):
                def evacs(i):
                    for h in range(NKV):
                        n = i * NKV + h
                        s.wait_ge(s_pe.sem, n + 1)
                        if n >= 2:
                            s.wait_ge(s_kvw[n % 2].sem, cnt2(n - 2))
                        s_ev.inc(s.activation(
                            out=kvev[:, (n % 2) * TW:(n % 2 + 1) * TW],
                            in_=kpsum[:, (n % 2) * TW:(n % 2 + 1) * TW],
                            func=AF.Copy))

                for pos0, (kind, ti) in enumerate(act_sched):
                    par = ti % 2
                    xa = xbuf[:, par * H:(par + 1) * H]
                    if kind == "sq":
                        s.wait_ge(s_x[ti % 2].sem, cnt2(ti))
                        if ti >= 1:
                            s.wait_ge(s_lnA.sem, ACT_IDX[("sq", ti - 1)])
                        if ti >= 2:
                            s.wait_ge(s_lnD.sem, (ti - 2) * LND + 4)
                        s_lnA.inc(s.activation(out=sqs[:], in_=xa, func=AF.Square,
                                               accum_out=stv("ss", par)))
                    else:
                        if ti == 0:
                            s.wait_ge(s_gp.sem, SETUP_GP)
                        s.wait_ge(s_lnD.sem, ti * LND + 4)
                        s_lnA.inc(s.activation(out=stv("r0", par), in_=stv("var", par),
                                               func=AF.Sqrt, bias=epscol[:], scale=1.0))
                        if ti % 4 == 3:
                            i = ti // 4
                            if i >= 1:
                                evacs(i - 1)
                evacs(NT - 1)
            @block.gpsimd
            def _(g):
                def xload(ti):
                    s_x[ti % 2].inc(g.dma_start(
                        out=xbuf[:, (ti % 2) * H:(ti % 2 + 1) * H],
                        in_=x_all[ti * 128:(ti + 1) * 128, :]))

                def ktw(i):
                    for h in range(NKV):
                        n = i * NKV + h
                        g.wait_ge(s_ev.sem, n + 1)
                        s_kvw[n % 2].inc(g.dma_start(
                            out=kt_d.ap()[h, :, i * TW:(i + 1) * TW],
                            in_=kvev[:, (n % 2) * TW:(n % 2 + 1) * TW]))

                xload(0)
                xload(1)
                for part in range(8):
                    s_wk.inc(g.dma_start(
                        out=wkb[:, part * 4096:(part + 1) * 4096].rearrange(
                            "p (c n) -> p c n", c=4),
                        in_=w_k[part * 512:(part + 1) * 512, :].rearrange(
                            "(c p) n -> p c n", p=128)))
                for ti in range(NTILE):
                    par = ti % 2
                    g.wait_ge(s_lnD.sem, ti * LND + 7)
                    g.wait_ge(s_lnA.sem, ACT_IDX[("sq", ti)])
                    if ti + 2 < NTILE:
                        xload(ti + 2)
                    s_xnw[par].inc(g.dma_start(
                        out=xn_tok.ap()[ti * 128:(ti + 1) * 128, :],
                        in_=xnb[:, par * H:(par + 1) * H]))
                    if ti % 4 == 3 and ti >= 7:
                        ktw(ti // 4 - 1)
                ktw(NT - 1)

            @block.tensor
            def _(t):
                for i in range(NT):
                    t.wait_ge(s_xnt[i % 2].sem, xr_cnt(i))
                    if i == 0:
                        t.wait_ge(s_wk.sem, 16 * 8)
                    par = i % 2
                    for h in range(NKV):
                        n = i * NKV + h
                        if n >= 2:
                            t.wait_ge(s_ev.sem, n - 1)
                        for hc in range(32):
                            ins = t.matmul(
                                kpsum[:, (n % 2) * TW:(n % 2 + 1) * TW],
                                wkb[:, (hc * NKV + h) * D:(hc * NKV + h + 1) * D],
                                xnt[:, (par * 32 + hc) * TW:(par * 32 + hc + 1) * TW],
                                start=(hc == 0), stop=(hc == 31))
                        s_pe.inc(ins)

        SNAP1 = snap_ctrs()
        K_PE = NT * NKV
        K_EV = K_PE
        K_KVW = K_PE

        if stage >= 2:
            with ExitStack() as ph:
                psb = lambda n, s, d: ph.enter_context(nc.sbuf_tensor(n, s, d))
                wvb = psb("wvb", [128, 32 * NKV * D], BF16)
                xnt = psb("xnt_v", [128, 2 * 32 * TW], BF16)
                kvev = psb("kvev_v", [128, 2 * TW], BF16)
                kpsum = ph.enter_context(nc.psum_tensor("vpsum", [128, 2 * TW], F32))

                def tread_v(eng, i, h0, h1):
                    if i >= 2:
                        eng.wait_ge(s_pe.sem, K_PE + (i - 1) * 8)
                    par = i % 2
                    for hc in range(h0, h1):
                        s_xnt[par].inc(eng.dma_start_transpose(
                            xnt[:, (par * 32 + hc) * TW:(par * 32 + hc + 1) * TW],
                            xn_tok.ap()[i * TW:(i + 1) * TW, hc * 128:(hc + 1) * 128]))

                @block.gpsimd
                def _(g):
                    fence(g, SNAP1)
                    for part in range(8):
                        s_wv.inc(g.dma_start(
                            out=wvb[:, part * 4096:(part + 1) * 4096].rearrange(
                                "p (c n) -> p c n", c=4),
                            in_=w_v[part * 512:(part + 1) * 512, :].rearrange(
                                "(c p) n -> p c n", p=128)))
                    with g.register("r_ta") as r_ta, g.register("r_tb") as r_tb:
                        g.reg_load(r_ta, itb[0:1, 10:11])
                        g.reg_load(r_tb, itb[0:1, 11:12])
                        ta = g.snap(r_ta, min_val=0, max_val=S - TW)
                        tb = g.snap(r_tb, min_val=0, max_val=S - TW)
                        s_bnc.inc(g.dma_start(out=xn_own_d.ap()[0:TW, :],
                                              in_=xn_tok.ap()[bass.ds(ta, TW), :]))
                        s_bnc.inc(g.dma_start(out=xn_own_d.ap()[TW:OWN, :],
                                              in_=xn_tok.ap()[bass.ds(tb, TW), :]))
                    for i in range(NT):
                        for ts in range(4):
                            for grp in range(2):
                                n = K_EV + i * 8 + ts * 2 + grp
                                g.wait_ge(s_ev.sem, n + 1)
                                s_kvw[n % 2].inc(g.dma_start(
                                    out=v_d.ap()[i * 4 + ts, :, grp * TW:(grp + 1) * TW],
                                    in_=kvev[:, (n % 2) * TW:(n % 2 + 1) * TW]))

                @block.sync
                def _(sync):
                    fence(sync, SNAP1)
                    for i in range(NT):
                        tread_v(sync, i, 0, 32)

                @block.tensor
                def _(t):
                    fence(t, SNAP1)
                    t.wait_ge(s_wv.sem, 16 * 8)
                    for i in range(NT):
                        t.wait_ge(s_xnt[i % 2].sem, xr_cnt(NT + i))
                        par = i % 2
                        for ts in range(4):
                            for grp in range(2):
                                n = K_PE + i * 8 + ts * 2 + grp
                                if n >= 2:
                                    t.wait_ge(s_ev.sem, n - 1)
                                for hc in range(32):
                                    xoff = (par * 32 + hc) * TW + ts * 128
                                    ins = t.matmul(
                                        kpsum[:, (n % 2) * TW:(n % 2 + 1) * TW],
                                        xnt[:, xoff:xoff + 128],
                                        wvb[:, (hc * NKV + grp * 4) * D:
                                            (hc * NKV + grp * 4 + 4) * D],
                                        start=(hc == 0), stop=(hc == 31))
                                s_pe.inc(ins)

                @block.scalar
                def _(s):
                    def evacs_v(i):
                        for ts in range(4):
                            for grp in range(2):
                                n = K_EV + i * 8 + ts * 2 + grp
                                s.wait_ge(s_pe.sem, n + 1)
                                if n >= 2:
                                    s.wait_ge(s_kvw[n % 2].sem, cnt2(n - 2))
                                s_ev.inc(s.activation(
                                    out=kvev[:, (n % 2) * TW:(n % 2 + 1) * TW],
                                    in_=kpsum[:, (n % 2) * TW:(n % 2 + 1) * TW],
                                    func=AF.Copy))
                    fence(s, SNAP1)
                    for i in range(NT):
                        evacs_v(i)

        SNAP2 = snap_ctrs()
        V_PE = K_PE + NT * 8
        V_EV = V_PE
        V_KVW = V_PE

        kt_s = nc.dram_tensor("kt_s", [NKV, 128, NCHUNK * 128], BF16, kind="Internal")
        v_s = nc.dram_tensor("v_s", [NCHUNK, 128, 2 * TW], BF16, kind="Internal")
        q_s = nc.dram_tensor("q_s", [NQH, 128, NSLOT * TW], BF16, kind="Internal")
        s_scp = [SEM(f"s_scp{i}", 16) for i in range(NSLOT)]
        SYMS = {}
        xnto = None
        if stage >= 2:
            xnto = ctx.enter_context(nc.sbuf_tensor("xnto", [128, 32 * OWN], BF16))
            with ExitStack() as ph:
                psb = lambda n, s, d: ph.enter_context(nc.sbuf_tensor(n, s, d))
                wqb = psb("wqb", [128, 2 * 32 * 8 * D], BF16)
                kvev = psb("kvev_q", [128, 2 * TW], BF16)
                kpsum = ph.enter_context(nc.psum_tensor("qpsum", [128, 2 * TW], F32))

                def tread_q(eng, t2, h0, h1):
                    eng.wait_ge(s_bnc.sem, 32)
                    for hc in range(h0, h1):
                        s_xnt[t2 % 2].inc(eng.dma_start_transpose(
                            xnto[:, hc * OWN + t2 * TW:hc * OWN + (t2 + 1) * TW],
                            xn_own_d.ap()[t2 * TW:(t2 + 1) * TW,
                                          hc * 128:(hc + 1) * 128]))

                @block.gpsimd
                def _(g):
                    fence(g, SNAP2)
                    def qwrites(qq):
                        for t2 in range(2):
                            for qh8 in range(8):
                                n = V_EV + qq * 16 + t2 * 8 + qh8
                                qh = qq * 8 + qh8
                                g.wait_ge(s_ev.sem, n + 1)
                                s_kvw[n % 2].inc(g.dma_start(
                                    out=q_d.ap()[qh, :, t2 * TW:(t2 + 1) * TW],
                                    in_=kvev[:, (n % 2) * TW:(n % 2 + 1) * TW]))

                    for qq in range(4):
                        if qq >= 2:
                            g.wait_ge(s_pe.sem, V_PE + (qq - 1) * 16)
                        for part in range(4):
                            s_wq[qq % 2].inc(g.dma_start(
                                out=wqb[:, ((qq % 2) * 32 + part * 8) * 1024:
                                        ((qq % 2) * 32 + part * 8 + 8) * 1024].rearrange(
                                    "p (c n) -> p c n", c=8),
                                in_=w_q[part * 1024:(part + 1) * 1024,
                                        qq * 1024:(qq + 1) * 1024].rearrange(
                                    "(c p) n -> p c n", p=128)))
                        if qq >= 1:
                            qwrites(qq - 1)
                    qwrites(3)

                @block.sync
                def _(sync):
                    fence(sync, SNAP2)
                    sync.wait_ge(s_itb.sem, 32)
                    for t2 in range(2):
                        tread_q(sync, t2, 0, 32)
                    rk = [sync.alloc_register(f"rkb{i}") for i in range(NSLOT)]
                    rq = [sync.alloc_register(f"rqo{i}") for i in range(NSLOT)]
                    kbs, kb128, qos = [], [], []
                    for i in range(NSLOT):
                        sync.reg_load(rk[i], itb[0:1, i:i + 1])
                        sync.reg_load(rq[i], itb[0:1, 5 + i:6 + i])
                        kbs.append(sync.snap(rk[i], min_val=0,
                                             max_val=64 - SLOT_CHUNKS[i]))
                        kb128.append(sync.snap(kbs[i] * 128, min_val=0,
                                               max_val=(64 - SLOT_CHUNKS[i]) * 128))
                        qos.append(sync.snap(rq[i], min_val=0, max_val=TW))
                    SYMS["qos"] = qos
                    spares = [sync.alloc_register(f"spare{i}") for i in range(16)]
                    first = [True]

                    def sym_dma(*a, **kw):
                        ins = sync.dma_start(*a, **kw)
                        if first[0]:
                            for sp in spares:
                                sync.free_register(sp)
                            first[0] = False
                        return ins

                    for sl in range(NSLOT):
                        cnt = SLOT_CHUNKS[sl]
                        sync.wait_ge(s_pe.sem, V_PE + 8 + sl * 12)
                        s_scp[sl].inc(sym_dma(
                            out=kt_s.ap()[:, :, SLOTK_OFF[sl] * 128:
                                          (SLOTK_OFF[sl] + cnt) * 128],
                            in_=kt_d.ap()[:, :, bass.ds(kb128[sl], cnt * 128)]))
                        s_scp[sl].inc(sym_dma(
                            out=v_s.ap()[SLOTK_OFF[sl]:SLOTK_OFF[sl] + cnt, :, :],
                            in_=v_d.ap()[bass.ds(kbs[sl], cnt), :, :]))

                @block.tensor
                def _(t):
                    fence(t, SNAP2)
                    for qq in range(4):
                        t.wait_ge(s_wq[qq % 2].sem, 16 * 4 * (qq // 2 + 1))
                        for t2 in range(2):
                            if qq == 0:
                                t.wait_ge(s_xnt[t2].sem, xr_cnt(2 * NT + t2))
                            for qh8 in range(8):
                                n = V_PE + qq * 16 + t2 * 8 + qh8
                                if n >= 2:
                                    t.wait_ge(s_ev.sem, n - 1)
                                for hc in range(32):
                                    ins = t.matmul(
                                        kpsum[:, (n % 2) * TW:(n % 2 + 1) * TW],
                                        wqb[:, (((qq % 2) * 32 + hc) * 8 + qh8) * D:
                                            (((qq % 2) * 32 + hc) * 8 + qh8 + 1) * D],
                                        xnto[:, hc * OWN + t2 * TW:hc * OWN + (t2 + 1) * TW],
                                        start=(hc == 0), stop=(hc == 31))
                                s_pe.inc(ins)

                @block.scalar
                def _(s):
                    fence(s, SNAP2)
                    for qq in range(4):
                        for t2 in range(2):
                            for qh8 in range(8):
                                n = V_EV + qq * 16 + t2 * 8 + qh8
                                s.wait_ge(s_pe.sem, n + 1)
                                if n >= 2:
                                    s.wait_ge(s_kvw[n % 2].sem, cnt2(n - 2))
                                s_ev.inc(s.activation(
                                    out=kvev[:, (n % 2) * TW:(n % 2 + 1) * TW],
                                    in_=kpsum[:, (n % 2) * TW:(n % 2 + 1) * TW],
                                    func=AF.Copy, scale=QSCALE))

        Q_PE = V_PE + 64
        Q_EV = Q_PE
        Q_KVW = Q_PE

        if stage < 3:
            @block.gpsimd
            def _(g):
                if stage >= 1:
                    g.wait_ge(s_xnw[0].sem, cnt2(NTILE - 2))
                    g.wait_ge(s_xnw[1].sem, cnt2(NTILE - 1))
                    s_out.inc(g.dma_start(out=dbg[0:128, 0:H],
                                          in_=xn_tok.ap()[256:384, :]))
                if stage >= 2:
                    g.wait_ge(s_kvw[0].sem, cnt2(Q_KVW - 2))
                    g.wait_ge(s_kvw[1].sem, cnt2(Q_KVW - 1))
                    s_out.inc(g.dma_start(out=dbg[0:128, H:H + 4096],
                                          in_=kt_d.ap()[2, :, 0:4096]))
                    s_out.inc(g.dma_start(out=dbg[0:128, 2 * H:2 * H + 1024],
                                          in_=v_d.ap()[7, :, :]))
                    s_out.inc(g.dma_start(out=dbg[0:128, 3 * H:3 * H + 1024],
                                          in_=q_d.ap()[5, :, :]))
                s_out.inc(g.dma_start(out=out_own[0:128, :],
                                      in_=xn_tok.ap()[0:128, :]))
                g.wait_ge(s_out.sem, s_out.n)
            return nc

        SNAP3 = snap_ctrs()

        pt_d = nc.dram_tensor("pt_d", [NQH, 128, OWN], BF16, kind="Internal")

        jobs = []
        for kv in range(NKV):
            for g_ in range(G):
                for sl in range(NSLOT):
                    jobs.append((kv, g_, sl))
        NJOB = len(jobs)
        NBAT = [SLOT_CHUNKS[sl] // 2 for (_, _, sl) in jobs]
        NBcum = [0]
        for b_ in NBAT:
            NBcum.append(NBcum[-1] + b_)

        s_ktl = [SEM("s_ktl0", 16), SEM("s_ktl1", 16)]
        s_vl = [SEM("s_vl0", 16), SEM("s_vl1", 16)]
        s_ql = [SEM("s_ql0", 16), SEM("s_ql1", 16)]
        s_el = [SEM("s_el0", 16), SEM("s_el1", 16)]
        s_sco = SEM("s_sco")
        s_exp = SEM("s_exp")
        s_mul = SEM("s_mul")
        s_pv = SEM("s_pv")
        s_drn = SEM("s_drn")
        s_fin = SEM("s_fin")
        s_rcp = SEM("s_rcp")
        s_pnv = SEM("s_pnv")
        s_tpx = SEM("s_tpx")
        s_tpe = SEM("s_tpe")
        s_ptw = [SEM("s_ptw0", 16), SEM("s_ptw1", 16)]
        s_p2i = SEM("s_p2i")

        with ExitStack() as ph:
            psb = lambda n, s, d: ph.enter_context(nc.sbuf_tensor(n, s, d))
            ktb = psb("ktb", [128, 2 * KMAX * 128], BF16)
            vab = psb("vab", [128, 2 * KMAX * 132], BF16)
            qsb = psb("qsb", [128, 2 * TW], BF16)
            esb = psb("esb", [128, 2 * EMAX], F32)
            ubuf = psb("ubuf", [128, 2 * 1024], F32)
            pbb = psb("pbb", [128, 4 * 1024], BF16)
            acc = psb("acc", [128, NSLOT * 520], F32)
            tfb = psb("tfb", [128, 2 * 520], F32)
            rcpb = psb("rcpb", [128, 8], F32)
            m20 = psb("m20", [128, 1], F32)
            pnv = psb("pnv", [128, 2 * 4096], BF16)
            ptb = psb("ptb", [128, 2 * 4096], BF16)
            spsum = ph.enter_context(nc.psum_tensor("spsum", [128, 2048], F32))
            pvps = ph.enter_context(nc.psum_tensor("pvps", [128, 1024], F32))
            tpp = [ph.enter_context(nc.psum_tensor("tpp0", [128, 128], BF16)),
                   ph.enter_context(nc.psum_tensor("tpp1", [128, 128], BF16))]

            def eview(pj, sl, b):
                cnt = SLOT_CHUNKS[sl]
                off = pj * EMAX + (cnt - 2 - 2 * b) * 128
                base = esb[:, off:off + 512]
                return bass.AP(base.tensor, base.offset,
                               [list(base.ap[0]), [128, 2], [1, 512]])

            def _p2_transposes(t, kvd):
                t.wait_ge(s_pnv.sem, (kvd + 1) * 32)
                for j in range(32):
                    if j >= 2 or kvd >= 1:
                        t.wait_ge(s_tpe.sem, kvd * 32 + j - 1)
                    s_tpx.inc(t.transpose(
                        tpp[j % 2][:],
                        pnv[:, (kvd % 2) * 4096 + j * 128:
                            (kvd % 2) * 4096 + (j + 1) * 128],
                        ident[:]))

            def _p2_tevacs(v, kvd):
                if kvd >= 2:
                    v.wait_ge(s_ptw[kvd % 2].sem, 16 * (kvd // 2))
                for j in range(32):
                    v.wait_ge(s_tpx.sem, kvd * 32 + j + 1)
                    s_tpe.inc(v.tensor_copy(
                        out=ptb[:, (kvd % 2) * 4096 + j * 128:
                                (kvd % 2) * 4096 + (j + 1) * 128],
                        in_=tpp[j % 2][:]))

            @block.gpsimd
            def _(g):
                fence(g, SNAP3)
                for n, (kv, g_, sl) in enumerate(jobs):
                    pj = n % 2
                    qh = kv * G + g_
                    if n >= 2:
                        g.wait_ge(s_mul.sem, NBcum[n - 1])
                    s_el[pj].inc(g.dma_start(
                        out=esb[:, pj * EMAX:pj * EMAX + REG_W[sl]],
                        in_=etab[qh, :, REG_OFF[sl]:REG_OFF[sl] + REG_W[sl]]))

            @block.sync
            def _(sync):
                fence(sync, SNAP3)
                sync.wait_ge(s_kvw[0].sem, cnt2(Q_KVW - 2))
                sync.wait_ge(s_kvw[1].sem, cnt2(Q_KVW - 1))
                qos = SYMS["qos"]
                for sl in range(NSLOT):
                    s_scp[sl].inc(sync.dma_start(
                        out=q_s.ap()[:, :, sl * TW:(sl + 1) * TW],
                        in_=q_d.ap()[:, :, bass.ds(qos[sl], TW)]))

                def ptwrite(kv):
                    sync.wait_ge(s_tpe.sem, (kv + 1) * 32)
                    s_ptw[kv % 2].inc(sync.dma_start(
                        out=pt_d.ap()[kv * G:(kv + 1) * G, :, :]
                            .transpose([1, 0, 2]),
                        in_=ptb[:, (kv % 2) * 4096:(kv % 2 + 1) * 4096]
                            .rearrange("p (a b) -> p a b", b=OWN)))

                for n, (kv, g_, sl) in enumerate(jobs):
                    cnt = SLOT_CHUNKS[sl]
                    pj = n % 2
                    qh = kv * G + g_
                    voff = (kv // 4) * 512 + (kv % 4) * 128
                    cs = SLOTK_OFF[sl]
                    if n % 20 == 1 and n // 20 >= 2:
                        ptwrite(n // 20 - 2)
                    if n >= 2:
                        sync.wait_ge(s_pv.sem, NBcum[n - 1])
                    if n < 2 * NSLOT:
                        sync.wait_ge(s_scp[sl].sem, 48)
                    s_ktl[pj].inc(sync.dma_start(
                        out=ktb[:, pj * KMAX * 128:pj * KMAX * 128 + cnt * 128],
                        in_=kt_s.ap()[kv, :, cs * 128:(cs + cnt) * 128]))
                    s_vl[pj].inc(sync.dma_start(
                        out=vab[:, pj * KMAX * 132:(pj * KMAX + cnt) * 132]
                            .rearrange("p (a b) -> p a b", b=132)[:, :, 0:128],
                        in_=v_s.ap()[cs:cs + cnt, :, voff:voff + 128]
                            .transpose([1, 0, 2])))
                    s_ql[pj].inc(sync.dma_start(
                        out=qsb[:, pj * TW:(pj + 1) * TW],
                        in_=q_s.ap()[qh, :, sl * TW:(sl + 1) * TW]))
                ptwrite(NKV - 2)
                ptwrite(NKV - 1)

            BATCHES = []
            for n, (kv, g_, sl) in enumerate(jobs):
                for b in range(SLOT_CHUNKS[sl] // 2):
                    BATCHES.append((n, b))

            @block.tensor
            def _(t):
                fence(t, SNAP3)
                t.wait_ge(s_p2i.sem, 2)

                def scores(nb):
                    n, b = BATCHES[nb]
                    _, _, sl = jobs[n]
                    pj = n % 2
                    pb = nb % 2
                    kbase = pj * KMAX * 128
                    if b == 0:
                        t.wait_ge(s_ktl[pj].sem, cnt2(n))
                        t.wait_ge(s_ql[pj].sem, cnt2(n))
                    if nb >= 2:
                        t.wait_ge(s_mul.sem, nb - 1)
                    for ci in range(2):
                        kc = 2 * b + 1 - ci
                        ins = t.matmul(
                            spsum[:, pb * 1024 + ci * 512:
                                  pb * 1024 + (ci + 1) * 512],
                            ktb[:, kbase + kc * 128:kbase + (kc + 1) * 128],
                            qsb[:, pj * TW:(pj + 1) * TW],
                            start=True, stop=True)
                    s_sco.inc(ins)

                def pv(nb):
                    n, b = BATCHES[nb]
                    _, _, sl = jobs[n]
                    B = SLOT_CHUNKS[sl] // 2
                    pj = n % 2
                    pb = nb % 4
                    if b == 0:
                        t.wait_ge(s_vl[pj].sem, cnt2(n))
                        t.wait_ge(s_drn.sem, n)
                    t.wait_ge(s_exp.sem, nb + 1)
                    for ci in range(2):
                        kc = 2 * b + 1 - ci
                        for sub in range(4):
                            ins = t.matmul(
                                pvps[:, sub * 256:sub * 256 + 129],
                                pbb[:, pb * 1024 + ci * 512 + sub * 128:
                                    pb * 1024 + ci * 512 + (sub + 1) * 128],
                                vab[:, (pj * KMAX + kc) * 132:
                                    (pj * KMAX + kc) * 132 + 129],
                                start=(b == 0 and ci == 0 and sub in (0, 2)),
                                stop=(b == B - 1 and ci == 1 and sub in (1, 3)))
                    s_pv.inc(ins)

                def pv_boundary(nb):
                    pv(nb)
                    np_, bp = BATCHES[nb]
                    if np_ % 20 == 19 and bp == NBAT[np_] - 1 and np_ // 20 >= 1:
                        _p2_transposes(t, np_ // 20 - 1)

                NB = len(BATCHES)
                for nb in range(NB):
                    scores(nb)
                    if nb >= 2:
                        pv_boundary(nb - 2)
                pv_boundary(NB - 2)
                pv_boundary(NB - 1)
                _p2_transposes(t, NKV - 1)

            @block.scalar
            def _(s):
                fence(s, SNAP3)
                s.wait_ge(s_p2i.sem, 2)
                nb = 0
                for n in range(NJOB):
                    for b in range(NBAT[n]):
                        pb = nb % 2
                        s.wait_ge(s_mul.sem, nb + 1)
                        if nb >= 4:
                            s.wait_ge(s_pv.sem, nb - 3)
                        s_exp.inc(s.activation(
                            out=pbb[:, (nb % 4) * 1024:(nb % 4 + 1) * 1024],
                            in_=ubuf[:, pb * 1024:(pb + 1) * 1024],
                            func=AF.Exp, bias=m20[:], scale=1.0))
                        nb += 1

            @block.vector
            def _(v):
                fence(v, SNAP3)
                s_p2i.inc(v.memset(m20[:], EXPB))
                s_p2i.inc(v.memset(
                    vab[:].rearrange("p (a b) -> p a b", b=132)[:, :, 128:129], 1.0))
                s_p2i.inc(v.memset(acc[:], 0.0))
                nb = 0
                for n, (kv, g_, sl) in enumerate(jobs):
                    cnt = SLOT_CHUNKS[sl]
                    B = cnt // 2
                    pj = n % 2
                    m = kv * G + g_
                    for b in range(B):
                        pb = nb % 2
                        v.wait_ge(s_sco.sem, nb + 1)
                        if b == 0:
                            v.wait_ge(s_el[pj].sem, cnt2(n))
                        if nb >= 2:
                            v.wait_ge(s_exp.sem, nb - 1)
                        s_mul.inc(v.tensor_tensor(
                            out=ubuf[:, pb * 1024:(pb + 1) * 1024]
                                .rearrange("p (a b) -> p a b", a=2),
                            in0=spsum[:, pb * 1024:(pb + 1) * 1024]
                                .rearrange("p (a b) -> p a b", a=2),
                            in1=eview(pj, sl, b), op=OP.add))
                        nb += 1
                    v.wait_ge(s_pv.sem, NBcum[n + 1])
                    if m >= 1:
                        v.wait_ge(s_fin.sem, m * 10)
                    s_drn.inc(v.tensor_copy(
                        out=acc[:, sl * 520:(sl + 1) * 520]
                            .rearrange("p (a b) -> p a b", b=130)[:, :, 0:129],
                        in_=pvps[:].rearrange("p (a b) -> p a b", b=256)[:, :, 0:129]))
                    if sl == NSLOT - 1:
                        v.wait_ge(s_drn.sem, 5 * m + 5)
                        if m >= 1:
                            v.wait_ge(s_pnv.sem, m * 8)
                            v.wait_ge(s_rcp.sem, m * 2)
                        if m == 0:
                            v.wait_ge(s_itb.sem, 32)
                        for tile in range(2):
                            base2 = m * 10 + tile * 5
                            s_fin.inc(v.tensor_scalar(
                                out=tfb[:, tile * 520:(tile + 1) * 520],
                                in0=acc[:, 0:520],
                                scalar1=ftb[:, tile * NSLOT:tile * NSLOT + 1],
                                scalar2=None, op0=OP.mult))
                            for sl2 in range(1, NSLOT):
                                v.wait_ge(s_fin.sem, base2 + sl2)
                                s_fin.inc(v.scalar_tensor_tensor(
                                    out=tfb[:, tile * 520:(tile + 1) * 520],
                                    in0=acc[:, sl2 * 520:(sl2 + 1) * 520],
                                    scalar=ftb[:, tile * NSLOT + sl2:
                                               tile * NSLOT + sl2 + 1],
                                    in1=tfb[:, tile * 520:(tile + 1) * 520],
                                    op0=OP.mult, op1=OP.add))
                        v.wait_ge(s_fin.sem, (m + 1) * 10)
                        for tile in range(2):
                            s_rcp.inc(v.reciprocal(
                                rcpb[:, tile * 4:(tile + 1) * 4]
                                    .rearrange("p (a b) -> p a b", b=1),
                                tfb[:, tile * 520:(tile + 1) * 520]
                                    .rearrange("p (a b) -> p a b", b=130)
                                    [:, :, 128:129]))
                        if kv >= 2 and g_ == 0:
                            v.wait_ge(s_tpx.sem, (kv - 1) * 32)
                        for tile in range(2):
                            for sub in range(4):
                                v.wait_ge(s_rcp.sem, m * 2 + tile + 1)
                                s_pnv.inc(v.tensor_scalar(
                                    out=pnv[:, (kv % 2) * 4096 +
                                            (g_ * 8 + tile * 4 + sub) * 128:
                                            (kv % 2) * 4096 +
                                            (g_ * 8 + tile * 4 + sub + 1) * 128],
                                    in0=tfb[:, tile * 520 + sub * 130:
                                            tile * 520 + sub * 130 + 128],
                                    scalar1=rcpb[:, tile * 4 + sub:
                                                 tile * 4 + sub + 1],
                                    scalar2=None, op0=OP.mult))
                    if n % 20 == 19 and n // 20 >= 1:
                        _p2_tevacs(v, n // 20 - 1)
                _p2_tevacs(v, NKV - 1)


        SNAP4 = snap_ctrs()

        if stage == 3:
            @block.gpsimd
            def _(g):
                g.wait_ge(s_ptw[0].sem, 16 * 4)
                g.wait_ge(s_ptw[1].sem, 16 * 4)
                s_out.inc(g.dma_start(out=dbg[0:128, 0:1024],
                                      in_=pt_d.ap()[2, :, :]))
                s_out.inc(g.dma_start(out=dbg[0:128, 1024:2048],
                                      in_=pt_d.ap()[17, :, :]))
                s_out.inc(g.dma_start(out=out_own[0:128, :],
                                      in_=xn_tok.ap()[0:128, :]))
                g.wait_ge(s_out.sem, s_out.n)
            return nc

        s_p3w = SEM("s_p3w", 16)
        s_p3i = SEM("s_p3i")
        s_gz = SEM("s_gz")
        s_p3d = SEM("s_p3d")
        s_ztx = SEM("s_ztx")
        s_zte = SEM("s_zte")
        s_ptl = SEM("s_ptl", 16)
        s_wo2 = [SEM("s_wo0", 16), SEM("s_wo1", 16)]
        s_p3g = SEM("s_p3g")
        s_p3pe = SEM("s_p3pe")
        s_epi = SEM("s_epi")
        s_xr = [SEM("s_xr0", 16), SEM("s_xr1", 16)]
        s_ost = [SEM("s_ost0", 16), SEM("s_ost1", 16)]
        P3D = 7

        with ExitStack() as ph:
            psb = lambda n, s, d: ph.enter_context(nc.sbuf_tensor(n, s, d))
            pts = psb("pts", [128, 32 * OWN], BF16)
            gwb = psb("gwb", [128, 128], BF16)
            lab = psb("lab", [128, 1024], BF16)
            gbb = psb("gbb", [1, 4], BF16)
            ones1 = psb("ones1", [1, 128], BF16)
            lbb = [psb(f"lbb{e}", [8, H], BF16) for e in range(4)]
            gzf = psb("gzf", [128, 40], F32)
            ohb = psb("ohb", [128, 5], F32)
            zmb = psb("zmb", [128, 8 * 32], BF16)
            ztb = [psb(f"ztb{e}", [8, 8 * 128], BF16) for e in range(4)]
            wob = psb("wob", [128, 2 * 4 * TW], BF16)
            wub = psb("wub", [128, 2 * 4 * TW], BF16)
            xrb = psb("xrb", [128, 2 * TW], F32)
            outb = psb("outb", [128, 2 * TW], F32)

            def _p3_ztrans(t, tsd):
                for e in range(4):
                    j = tsd * 4 + e
                    t.wait_ge(s_p3d.sem, tsd * P3D + P3D)
                    if j >= 2:
                        t.wait_ge(s_zte.sem, j - 1)
                    s_ztx.inc(t.transpose(
                        tpz[j % 2][:],
                        zmb[:, tsd * 32 + e * 8:tsd * 32 + (e + 1) * 8],
                        ident[:]))

            def _p3_ztevac(v, tsd):
                for e in range(4):
                    j = tsd * 4 + e
                    v.wait_ge(s_ztx.sem, j + 1)
                    s_zte.inc(v.tensor_copy(
                        out=ztb[e][:, tsd * 128:(tsd + 1) * 128],
                        in_=tpz[j % 2][:]))

            with ExitStack() as ph2:
                psg = ph2.enter_context(nc.psum_tensor("psg", [128, 64], F32))
                tpz = [ph2.enter_context(nc.psum_tensor("tpz0", [8, 128], BF16)),
                       ph2.enter_context(nc.psum_tensor("tpz1", [8, 128], BF16))]

                @block.gpsimd
                def _(g):
                    fence(g, SNAP4)
                    s_p3w.inc(g.dma_start(out=gwb[:], in_=gate_wt))
                    s_p3w.inc(g.dma_start(out=lab[:], in_=lora_at))
                    s_p3w.inc(g.dma_start(out=gbb[:], in_=gate_b))
                    for e in range(4):
                        s_p3w.inc(g.dma_start(out=lbb[e][:],
                                              in_=lora_bt[e * 8:(e + 1) * 8, :]))
                    s_p3i.inc(g.memset(ones1[:], 1.0))

                @block.sync
                def _(sync):
                    fence(sync, SNAP4)
                    sync.wait_ge(s_ptw[0].sem, 16 * 4)
                    sync.wait_ge(s_ptw[1].sem, 16 * 4)
                    for kv8 in range(4):
                        s_ptl.inc(sync.dma_start(
                            out=pts[:, kv8 * 8 * OWN:(kv8 + 1) * 8 * OWN]
                                .rearrange("p (a b) -> p a b", b=OWN),
                            in_=pt_d.ap()[kv8 * 8:(kv8 + 1) * 8, :, :]
                                .transpose([1, 0, 2])))

                @block.tensor
                def _(t):
                    fence(t, SNAP4)
                    t.wait_ge(s_p3w.sem, 16 * 7)
                    t.wait_ge(s_p3i.sem, 1)
                    for ts in range(8):
                        if ts >= 1:
                            t.wait_ge(s_p3d.sem, (ts - 1) * P3D + 1)
                        for hc in range(32):
                            t.matmul(psg[:, 0:4],
                                     xnto[:, hc * OWN + ts * 128:
                                          hc * OWN + (ts + 1) * 128],
                                     gwb[:, hc * 4:(hc + 1) * 4],
                                     start=(hc == 0), stop=False)
                        t.matmul(psg[:, 0:4],
                                 ones1[:], gbb[:], start=False, stop=True)
                        for hc in range(32):
                            ins = t.matmul(
                                psg[:, 4:36],
                                xnto[:, hc * OWN + ts * 128:
                                     hc * OWN + (ts + 1) * 128],
                                lab[:, hc * 32:(hc + 1) * 32],
                                start=(hc == 0), stop=(hc == 31))
                        s_gz.inc(ins)
                        if ts >= 1:
                            _p3_ztrans(t, ts - 1)
                    _p3_ztrans(t, 7)

                @block.vector
                def _(v):
                    fence(v, SNAP4)
                    for ts in range(8):
                        base = ts * P3D
                        v.wait_ge(s_gz.sem, ts + 1)
                        if ts >= 1:
                            v.wait_ge(s_p3d.sem, base)
                        s_p3d.inc(v.tensor_copy(gzf[:, 0:36], psg[:, 0:36]))
                        v.wait_ge(s_p3d.sem, base + 1)
                        s_p3d.inc(v.reduce_max(ohb[:, 4:5], gzf[:, 0:4], axis=AX))
                        v.wait_ge(s_p3d.sem, base + 2)
                        s_p3d.inc(v.tensor_scalar(
                            out=ohb[:, 0:4], in0=gzf[:, 0:4],
                            scalar1=ohb[:, 4:5], scalar2=None, op0=OP.is_equal))
                        for e in range(4):
                            v.wait_ge(s_p3d.sem, base + 3)
                            s_p3d.inc(v.tensor_scalar(
                                out=zmb[:, ts * 32 + e * 8:ts * 32 + (e + 1) * 8],
                                in0=gzf[:, 4 + e * 8:4 + (e + 1) * 8],
                                scalar1=ohb[:, e:e + 1], scalar2=None, op0=OP.mult))
                        if ts >= 1:
                            _p3_ztevac(v, ts - 1)
                    _p3_ztevac(v, 7)

            with ExitStack() as ph3:
                pso = ph3.enter_context(nc.psum_tensor("pso", [128, 8 * TW], F32))

                @block.gpsimd
                def _(g):
                    for fs in range(8):
                        for hcg in range(8):
                            gw = fs * 8 + hcg
                            if gw >= 2:
                                g.wait_ge(s_p3g.sem, gw - 1)
                            for w_i, (wt, wbuf) in enumerate(((w_o, wob),
                                                              (w_up, wub))):
                                s_wo2[gw % 2].inc(g.dma_start(
                                    out=wbuf[:, ((gw % 2) * 4) * TW:
                                             ((gw % 2) * 4 + 4) * TW]
                                        .rearrange("p (c n) -> p c n", c=4),
                                    in_=wt[hcg * 512:(hcg + 1) * 512,
                                           fs * TW:(fs + 1) * TW]
                                        .rearrange("(c p) n -> p c n", p=128)))

                @block.sync
                def _(sync):
                    for k in range(64):
                        fs, ts = k // 8, k % 8
                        if k >= 2:
                            sync.wait_ge(s_epi.sem, k - 1)
                        s_xr[k % 2].inc(sync.dma_start(
                            out=xrb[:, (k % 2) * TW:(k % 2 + 1) * TW],
                            in_=x_own[ts * 128:(ts + 1) * 128,
                                      fs * TW:(fs + 1) * TW]))
                        if k >= 2:
                            k2 = k - 2
                            sync.wait_ge(s_epi.sem, k2 + 1)
                            s_ost[k2 % 2].inc(sync.dma_start(
                                out=out_own[(k2 % 8) * 128:(k2 % 8 + 1) * 128,
                                            (k2 // 8) * TW:(k2 // 8 + 1) * TW],
                                in_=outb[:, (k2 % 2) * TW:(k2 % 2 + 1) * TW]))
                    for k2 in (62, 63):
                        sync.wait_ge(s_epi.sem, k2 + 1)
                        s_ost[k2 % 2].inc(sync.dma_start(
                            out=out_own[(k2 % 8) * 128:(k2 % 8 + 1) * 128,
                                        (k2 // 8) * TW:(k2 // 8 + 1) * TW],
                            in_=outb[:, (k2 % 2) * TW:(k2 % 2 + 1) * TW]))
                    sync.wait_ge(s_ost[0].sem, 16 * 32)
                    sync.wait_ge(s_ost[1].sem, 16 * 32)

                @block.tensor
                def _(t):
                    t.wait_ge(s_ptl.sem, 16 * 4)
                    t.wait_ge(s_zte.sem, 32)
                    t.wait_ge(s_p3d.sem, 7 * P3D + 1)
                    for fs in range(8):
                        for hcg in range(8):
                            gw = fs * 8 + hcg
                            t.wait_ge(s_wo2[gw % 2].sem, 32 * (gw // 2 + 1))
                            last = None
                            for w_i, wbuf in enumerate((wob, wub)):
                                src = pts if w_i == 0 else xnto
                                for hc4 in range(4):
                                    hc = hcg * 4 + hc4
                                    for ts in range(8):
                                        if (hcg == 0 and w_i == 0 and hc4 == 0
                                                and fs >= 1):
                                            t.wait_ge(s_epi.sem,
                                                      (fs - 1) * 8 + ts + 1)
                                        last = t.matmul(
                                            pso[:, ts * TW:(ts + 1) * TW],
                                            src[:, hc * OWN + ts * 128:
                                                hc * OWN + (ts + 1) * 128],
                                            wbuf[:, ((gw % 2) * 4 + hc4) * TW:
                                                 ((gw % 2) * 4 + hc4 + 1) * TW],
                                            start=(hcg == 0 and w_i == 0
                                                   and hc4 == 0),
                                            stop=False)
                            s_p3g.inc(last)
                        for ts in range(8):
                            for e in range(4):
                                ins = t.matmul(
                                    pso[:, ts * TW:(ts + 1) * TW],
                                    ztb[e][:, ts * 128:(ts + 1) * 128],
                                    lbb[e][:, fs * TW:(fs + 1) * TW],
                                    start=False, stop=(e == 3))
                            s_p3pe.inc(ins)

                @block.vector
                def _(v):
                    for k in range(64):
                        fs, ts = k // 8, k % 8
                        v.wait_ge(s_p3pe.sem, k + 1)
                        v.wait_ge(s_xr[k % 2].sem, cnt2(k))
                        if k >= 2:
                            v.wait_ge(s_ost[k % 2].sem, cnt2(k - 2))
                        s_epi.inc(v.tensor_tensor(
                            out=outb[:, (k % 2) * TW:(k % 2 + 1) * TW],
                            in0=pso[:, ts * TW:(ts + 1) * TW],
                            in1=xrb[:, (k % 2) * TW:(k % 2 + 1) * TW],
                            op=OP.add))

    return nc



def _host_inputs(inputs, c):
    x = np.ascontiguousarray(inputs["L0_attn_hidden_states"].reshape(S, H),
                             dtype=np.float32)
    w_qkv = np.ascontiguousarray(inputs["L0_attn_w_qkv"], dtype=np.float32)
    tiles = (c, 15 - c)
    tile_of, kbase, mask = _plan(c)
    x_own = np.ascontiguousarray(
        np.concatenate([x[t * TW:(t + 1) * TW] for t in tiles], axis=0))
    gate_w = np.ascontiguousarray(inputs["L0_ffn_gate_w"], dtype=np.float32)
    gwt = np.ascontiguousarray(
        gate_w.T.reshape(32, 128, 4).transpose(1, 0, 2).reshape(128, 128))
    lora_A = np.ascontiguousarray(inputs["L0_ffn_lora_A"], dtype=np.float32)
    lat = np.ascontiguousarray(
        lora_A.reshape(32, H).T.reshape(32, 128, 32).transpose(1, 0, 2).reshape(128, 1024))
    lora_B = np.ascontiguousarray(inputs["L0_ffn_lora_B"], dtype=np.float32)
    lbt = np.ascontiguousarray(lora_B.transpose(0, 2, 1).reshape(32, H))
    it = np.zeros((1, 16), np.int32)
    it[0, 0:5] = kbase
    it[0, 5:10] = [tile_of[s] * TW for s in range(NSLOT)]
    it[0, 10] = tiles[0] * TW
    it[0, 11] = tiles[1] * TW
    ft = np.zeros((128, 16), np.float32)
    for t in range(2):
        for s in range(NSLOT):
            ft[:, t * NSLOT + s] = mask[t][s]
    return {
        "x_all": x,
        "x_own": x_own,
        "w_q": np.ascontiguousarray(w_qkv[:, 0:NQH * D]),
        "w_k": np.ascontiguousarray(w_qkv[:, NQH * D:NQH * D + NKV * D]),
        "w_v": np.ascontiguousarray(w_qkv[:, NQH * D + NKV * D:]),
        "w_o": np.ascontiguousarray(inputs["L0_attn_w_o"], dtype=np.float32),
        "w_up": np.ascontiguousarray(
            np.asarray(inputs["L0_ffn_W_up"], dtype=np.float32).T),
        "gate_wt": gwt,
        "gate_b": np.ascontiguousarray(
            inputs["L0_ffn_expert_bias"].reshape(1, 4), dtype=np.float32),
        "lora_at": lat,
        "lora_bt": lbt,
        "etab": _bake_etab(c),
        "itab": it,
        "ftab": ft,
    }


def kernel(**inputs):
    nc = build_nc(stage=4)
    in_maps = [_host_inputs(inputs, c) for c in range(NCORE)]
    res = run_bass_kernel_spmd(nc, in_maps, core_ids=list(range(NCORE)))
    out = np.empty((S, H), dtype=np.float32)
    for c in range(NCORE):
        o = res.results[c]["out_own"]
        out[c * TW:(c + 1) * TW] = o[0:TW]
        out[(15 - c) * TW:(16 - c) * TW] = o[TW:OWN]
    return out.reshape(1, S, H)
